# version 07
# baseline (speedup 1.0000x reference)
import os
import numpy as np
import ml_dtypes
import concourse.bass as bass
import concourse.mybir as mybir
from concourse.bass_utils import run_bass_kernel_spmd

F32 = mybir.dt.float32
BF16 = mybir.dt.bfloat16
U8 = mybir.dt.uint8
AF = mybir.ActivationFunctionType
ALU = mybir.AluOpType
AXX = mybir.AxisListType.X

B, T, D = 4, 4096, 4096
NCORES = 8
TPC = 2048
NH, NKV, HD = 32, 8, 128
NTB, NTCH = 16, 4
SCALE = 1.0 / float(np.sqrt(128.0))
EPS = 1e-5
CAPACITY = 6144.0
INV_D = 1.0 / 4096.0
ENGINES = ("sync", "tensor", "vector", "scalar", "gpsimd")


class S:
    __slots__ = ("h", "n")

    def __init__(self, h):
        self.h = h
        self.n = 0

    def inc(self, k=1):
        self.n += k
        return self.n


class KB:
    def __init__(self, dbg=()):
        self.nc = bass.Bass("TRN2", target_bir_lowering=False, debug=False)
        self.ops = {e: [] for e in ENGINES}
        self.dbg = set(dbg)

    def em(self, e, meth, *args, inc=None, **kw):
        self.ops[e].append((meth, args, kw, inc or []))

    def wait(self, e, s, v=None):
        self.em(e, "wait_ge", s.h, s.n if v is None else v)

    def dma(self, e, out, in_, s):
        self.em(e, "dma_start", inc=[(s, 16)], out=out, in_=in_)
        return s.inc(16)

    def fence(self):
        self.wait("sync", self.sQX)
        self.wait("gpsimd", self.sQW)
        self.wait("scalar", self.sQS)
        tgt = self.sFE.n + len(ENGINES)
        for e in ENGINES:
            self.em(e, "nop", inc=[(self.sFE, 1)])
        self.sFE.n = tgt
        for e in ENGINES:
            self.wait(e, self.sFE, tgt)

    def finalize(self):
        ops = self.ops

        with self.nc.Block() as block:
            def run_for(name):
                def body(eng):
                    for meth, args, kw, incs in ops[name]:
                        ins = getattr(eng, meth)(*args, **kw)
                        for (sm, v) in incs:
                            ins = ins.then_inc(sm.h, v)
                return body

            block.sync(run_for("sync"))
            block.tensor(run_for("tensor"))
            block.vector(run_for("vector"))
            block.scalar(run_for("scalar"))
            block.gpsimd(run_for("gpsimd"))
        return self.nc

    def sb(self, name, shape, dtype):
        return self.nc.alloc_sbuf_tensor(name, shape, dtype)

    def carve(self, region, off_bytes, dtype, cols, parts=128):
        esz = 2 if dtype == BF16 else 4
        assert off_bytes % esz == 0
        h = region.bitcast(dtype)
        ap = h.ap()[0:parts, off_bytes // esz: off_bytes // esz + cols]
        return ap

    def tap(self, name, ap, shape, dtype):
        if name not in self.dbg:
            return
        t = self.nc.dram_tensor("dbg_" + name, shape, dtype, kind="ExternalOutput").ap()
        self.wait("sync", self.sVE)
        self.wait("sync", self.sSE)
        self.dma("sync", t, ap, self.sQX)


def build_program(dbg=(), upto="E"):
    k = KB(dbg)
    nc = k.nc

    def din(name, shape, dtype=F32):
        return nc.dram_tensor(name, shape, dtype, kind="ExternalInput")

    xq = din("xq", [TPC, D])
    xkv = din("xkv", [TPC, D])
    wq = din("wq", [D, D])
    wk = din("wk", [D, 1024])
    wv = din("wv", [D, 1024])
    wo = din("wo", [D, D])
    wffn = din("wffn", [D, D])
    astack = din("astack", [D, 128])
    bstack = din("bstack", [128, D])
    gw1 = din("gw1", [32, D])
    gw2 = din("gw2", [8, 32])
    gvec = din("gvec", [1, 104])
    cxq = din("cxq", [128, TPC], BF16)
    ssq = din("ssq", [128, TPC], BF16)
    cxk = din("cxk", [128, TPC], BF16)
    ssk = din("ssk", [128, TPC], BF16)
    masks = din("masks", [8, 128, 512], BF16)
    identb = din("identb", [128, 128], BF16)
    identf = din("identf", [128, 128], F32)
    onesb = din("onesb", [128, 1], BF16)
    onesfr = din("onesfr", [1, 128], F32)
    erep = din("erep", [8, 128], BF16)
    epsd = din("epsd", [128, 1], F32)
    out_d = nc.dram_tensor("out", [TPC, D], F32, kind="ExternalOutput")

    def dint(name, shape, dtype=BF16):
        return nc.dram_tensor(name, shape, dtype, kind="Internal")

    qscr = dint("qscr", [NH, NTCH, 128, 512])
    kcc_in = dint("kcc_in", [NKV * NTCH * 128, 512])
    kcc_out = dint("kcc_out", [2 * NKV * NTCH * 128, 512])
    vcc_in = dint("vcc_in", [NKV * 16 * 128, 128])
    vcc_out = dint("vcc_out", [2 * NKV * 16 * 128, 128])
    ctxscr = dint("ctxscr", [NH, NTCH, 128, 512])
    aoscr = dint("aoscr", [32, NTCH, 128, 512])
    x1scr = dint("x1scr", [NTB, 128, D])
    foscr = dint("foscr", [32, NTCH, 128, 512])
    cnt_in = dint("cnt_in", [1, 16], F32)
    cnt_out = dint("cnt_out", [1, 16], F32)

    kcc4 = kcc_in.reshape([NKV, NTCH, 128, 512])
    kcc_o4 = kcc_out.reshape([2, NKV, NTCH, 128, 512])
    vcc4 = vcc_in.reshape([NKV, 16, 128, 128])
    vcc_o4 = vcc_out.reshape([2, NKV, 16, 128, 128])

    for s_ in ("sQX", "sQW", "sQS", "sTM", "sVE", "sSE", "sFE", "sCK", "sCV", "sCN"):
        setattr(k, s_, S(nc.alloc_semaphore(s_)))
    sQX, sQW, sQS = k.sQX, k.sQW, k.sQS
    sTM, sVE, sSE = k.sTM, k.sVE, k.sSE
    sCK, sCV, sCN = k.sCK, k.sCV, k.sCN

    idb = k.sb("idb", [128, 128], BF16).ap()
    idf = k.sb("idf", [128, 128], F32).ap()
    onb = k.sb("onb", [128, 1], BF16).ap()
    onfr = k.sb("onfr", [1, 128], F32).ap()
    erp = k.sb("erp", [8, 128], BF16).ap()
    epsS = k.sb("epsS", [128, 1], F32).ap()
    gw1T = k.sb("gw1T", [128, 32 * 32], BF16)
    gw2T = k.sb("gw2T", [32, 8], BF16).ap()
    gvb = k.sb("gvb", [128, 104], F32).ap()
    issb = k.sb("issb", [128, NTB * 16], BF16)
    g12 = k.sb("g12", [128, NTB * 2], F32)
    cntS = k.sb("cntS", [1, 16], F32).ap()
    alwS = k.sb("alwS", [1, 16], F32).ap()
    alwB = k.sb("alwB", [128, 16], F32).ap()
    wTall = k.sb("wTall", [32, NTB * 128], BF16)
    st = k.sb("stats", [128, 16], F32)
    S1t, S2t, Mt, MSQt, VARt, SQt, RSt = (st.ap()[:, i:i + 1] for i in range(7))

    ARENA_BYTES = 131072
    SIDE_BYTES = 65536
    arena = k.sb("arena", [128, ARENA_BYTES], U8)
    side = k.sb("side", [128, SIDE_BYTES], U8)
    assert nc.sbuf_bytes_remaining > 2048, nc.sbuf_bytes_remaining

    def AR(off, dtype, cols, parts=128):
        return k.carve(arena, off, dtype, cols, parts)

    def SD(off, dtype, cols, parts=128):
        return k.carve(side, off, dtype, cols, parts)

    xlnT = AR(0, BF16, 32 * 2048)
    ctxT = AR(0, BF16, 32 * 2048)
    xfT = AR(0, BF16, 32 * 2048)

    def xlnT_s(i, c0, n):
        return xlnT[:, i * 2048 + c0: i * 2048 + c0 + n]

    p = [nc.alloc_psum_tensor("p%d" % i, [128, 512], F32) for i in range(5)]
    p5 = nc.alloc_psum_tensor("p5", [128, 1024], BF16)
    p6 = nc.alloc_psum_tensor("p6", [128, 512], F32)
    p7 = nc.alloc_psum_tensor("p7", [128, 512], F32)
    assert nc.psum_banks_remaining == 0 or nc.psum_banks_remaining >= 0

    k.dma("sync", idb, identb.ap(), sQX)
    k.dma("sync", idf, identf.ap(), sQX)
    k.dma("sync", onb, onesb.ap(), sQX)
    k.dma("sync", onfr, onesfr.ap(), sQX)
    k.dma("sync", erp, erep.ap(), sQX)
    k.dma("sync", epsS, epsd.ap(), sQX)
    k.fence()
    if upto == "P0":
        return k

    x0f2b = [SD(0, F32, 4096), SD(38912, F32, 4096)]
    xlnb = SD(16384, BF16, 4096)
    junkA = SD(55296, BF16, 4096)
    wstr = [SD(0, BF16, 32 * 128), SD(8192, BF16, 32 * 128)]
    accT = SD(16384, BF16, 2048)
    ro_t1 = SD(20480, BF16, 2048)
    cxS = SD(24576, BF16, 2048)
    ssS = SD(28672, BF16, 2048)
    vtmp = SD(32768, BF16, 512)
    vtok = SD(33792, BF16, 512)
    qsw = SD(34816, BF16, 2048)

    def wstr_tile(buf, i):
        return wstr[buf][:, i * 128:(i + 1) * 128]

    def ln_build(src_dram, tag):
        ve_xln_done = [0] * NTB
        tm_tr_done = [0] * NTB
        for tb in range(NTB):
            x0f = x0f2b[tb % 2]
            if tb > 1:
                k.wait("sync", sVE, ve_xln_done[tb - 2])
            ldv = k.dma("sync", x0f, src_dram.ap()[tb * 128:(tb + 1) * 128, :], sQX)
            k.wait("scalar", sQX, ldv)
            k.em("scalar", "activation", junkA, x0f, AF.Square, inc=[(sSE, 1)],
                 accum_out=S2t)
            se_sq = sSE.inc()
            k.wait("vector", sQX, ldv)
            k.em("vector", "reduce_sum", S1t, x0f, axis=AXX, inc=[(sVE, 1)])
            sVE.inc()
            k.wait("vector", sVE)
            k.em("vector", "tensor_scalar", Mt, S1t, INV_D, None, ALU.mult,
                 inc=[(sVE, 1)])
            sVE.inc()
            k.wait("vector", sVE)
            k.em("vector", "tensor_scalar", MSQt, Mt, Mt, None, ALU.mult,
                 inc=[(sVE, 1)])
            sVE.inc()
            k.wait("vector", sSE, se_sq)
            k.em("vector", "tensor_scalar", VARt, S2t, INV_D, None, ALU.mult)
            k.wait("vector", sVE)
            k.em("vector", "tensor_tensor", VARt, VARt, MSQt, ALU.subtract,
                 inc=[(sVE, 1)])
            sVE.inc()
            k.wait("scalar", sVE)
            k.em("scalar", "activation", SQt, VARt, AF.Sqrt, bias=epsS,
                 inc=[(sSE, 1)])
            sSE.inc()
            k.wait("vector", sSE)
            k.em("vector", "reciprocal", RSt, SQt, inc=[(sVE, 1)])
            sVE.inc()
            k.wait("vector", sVE)
            if tb > 0:
                k.wait("vector", sTM, tm_tr_done[tb - 1])
            k.em("vector", "tensor_scalar", xlnb, x0f, Mt, RSt,
                 ALU.subtract, ALU.mult, inc=[(sVE, 1)])
            ve_xln_done[tb] = sVE.inc()
            for grp in range(8):
                half = grp % 2
                k.wait("tensor", sVE)
                k.wait("tensor", sVE, ve_xln_done[tb])
                for q in range(4):
                    i = grp * 4 + q
                    k.em("tensor", "transpose",
                         p5[:, half * 512 + q * 128: half * 512 + (q + 1) * 128],
                         xlnb[:, i * 128:(i + 1) * 128], idb,
                         inc=[(sTM, 1)] if q == 3 else None)
                tm_tr_done[tb] = sTM.inc()
                k.wait("vector", sTM, tm_tr_done[tb])
                dst3 = xlnT.rearrange("p (a t) -> p a t", t=2048)[
                    :, grp * 4:(grp + 1) * 4, tb * 128:(tb + 1) * 128]
                src3 = p5[:, half * 512:(half + 1) * 512].rearrange(
                    "p (a t) -> p a t", a=4)
                k.em("vector", "tensor_copy", dst3, src3, inc=[(sVE, 1)])
                sVE.inc()
        k.fence()

    def proj_phase(w_dram, n_oc, oc_base_off, store_kind):
        pass

    ln_build(xkv, "kv")
    ldcx = k.dma("sync", cxS, cxk.ap(), sQX)
    ldss = k.dma("sync", ssS, ssk.ap(), sQX)

    def do_proj(oc_idx, w_dram, w_col0, is_k, is_v, is_q, head_idx):
        buf = oc_idx % 2
        if do_proj.tm_done[buf] > 0:
            k.wait("gpsimd", sTM, do_proj.tm_done[buf])
        src = w_dram.ap()[:, w_col0:w_col0 + 128].rearrange(
            "(a p) c -> p a c", p=128)
        dst = wstr[buf][:, :].rearrange("p (a c) -> p a c", c=128)
        ldw = k.dma("gpsimd", dst, src, sQW)
        for tc in range(NTCH):
            bank = p[(oc_idx * NTCH + tc) % 2]
            if do_proj.se_bank[(oc_idx * NTCH + tc) % 2] > 0:
                k.wait("tensor", sSE, do_proj.se_bank[(oc_idx * NTCH + tc) % 2])
            k.wait("tensor", sQW, ldw)
            for i in range(32):
                k.em("tensor", "matmul", bank[:], wstr_tile(buf, i),
                     xlnT_s(i, tc * 512, 512),
                     start=(i == 0), stop=(i == 31),
                     inc=[(sTM, 1)] if i == 31 else None)
            mm_done = sTM.inc()
            if tc == NTCH - 1:
                do_proj.tm_done[buf] = mm_done
            k.wait("scalar", sTM, mm_done)
            if is_v:
                k.em("scalar", "activation", vtmp, bank[:], AF.Copy,
                     inc=[(sSE, 1)])
                do_proj.se_bank[(oc_idx * NTCH + tc) % 2] = sSE.inc()
                k.wait("tensor", sSE)
                if do_proj.v_ev > 0:
                    k.wait("tensor", sVE, do_proj.v_ev)
                for q in range(4):
                    k.em("tensor", "transpose", p5[:, q * 128:(q + 1) * 128],
                         vtmp[:, q * 128:(q + 1) * 128], idb,
                         inc=[(sTM, 1)] if q == 3 else None)
                ttr = sTM.inc()
                k.wait("vector", sTM, ttr)
                if do_proj.v_st > 0:
                    k.wait("vector", sQS, do_proj.v_st)
                k.em("vector", "tensor_copy", vtok, p5[:, 0:512],
                     inc=[(sVE, 1)])
                do_proj.v_ev = sVE.inc()
                k.wait("scalar", sVE, do_proj.v_ev)
                dstv = vcc4[head_idx, tc * 4:(tc + 1) * 4].rearrange(
                    "a p d -> p a d")
                srcv = vtok.rearrange("p (a d) -> p a d", a=4)
                do_proj.v_st = k.dma("scalar", dstv, srcv, sQS)
            else:
                if tc == 0 and do_proj.acc_busy > 0:
                    k.wait("scalar", sQS, do_proj.acc_busy)
                k.em("scalar", "activation", accT[:, tc * 512:(tc + 1) * 512],
                     bank[:], AF.Copy, inc=[(sSE, 1)])
                do_proj.se_bank[(oc_idx * NTCH + tc) % 2] = sSE.inc()
        if is_v:
            return
        k.wait("scalar", sSE)
        k.dma("scalar", qsw[0:64, :], accT[64:128, :], sQS)
        swv = k.dma("scalar", qsw[64:128, :], accT[0:64, :], sQS)
        k.wait("vector", sQS, swv)
        k.em("vector", "tensor_tensor", ro_t1, qsw, ssS, ALU.mult,
             inc=[(sVE, 1)])
        sVE.inc()
        k.em("vector", "tensor_tensor", accT, accT, cxS, ALU.mult,
             inc=[(sVE, 1)])
        sVE.inc()
        k.wait("vector", sVE)
        k.em("vector", "tensor_tensor", accT, accT, ro_t1, ALU.add,
             inc=[(sVE, 1)])
        rope_done = sVE.inc()
        k.wait("scalar", sVE, rope_done)
        if is_q:
            for tc in range(NTCH):
                do_proj.acc_busy = k.dma(
                    "scalar", qscr[head_idx, tc],
                    accT[:, tc * 512:(tc + 1) * 512], sQS)
        else:
            for tc in range(NTCH):
                do_proj.acc_busy = k.dma(
                    "scalar", kcc4[head_idx, tc],
                    accT[:, tc * 512:(tc + 1) * 512], sQS)

    do_proj.tm_done = [0, 0]
    do_proj.se_bank = [0, 0]
    do_proj.acc_busy = 0
    do_proj.v_ev = 0
    do_proj.v_st = 0

    k.wait("vector", sQX, ldss)
    for g in range(NKV):
        do_proj(g, wk, g * 128, True, False, False, g)
    for g in range(NKV):
        do_proj(NKV + g, wv, g * 128, False, True, False, g)
    k.wait("gpsimd", sQS)
    k.em("gpsimd", "collective_compute", "AllGather", ALU.bypass,
         [[0, 1], [2, 3], [4, 5], [6, 7]],
         [kcc_in.ap().opt()], [kcc_out.ap().opt()], inc=[(sCK, 1)])
    sCK.inc()
    k.em("gpsimd", "collective_compute", "AllGather", ALU.bypass,
         [[0, 1], [2, 3], [4, 5], [6, 7]],
         [vcc_in.ap().opt()], [vcc_out.ap().opt()], inc=[(sCV, 1)])
    sCV.inc()
    k.fence()
    if upto == "A1":
        return k

    ln_build(xq, "q")
    ldcx = k.dma("sync", cxS, cxq.ap(), sQX)
    ldss = k.dma("sync", ssS, ssq.ap(), sQX)
    k.wait("vector", sQX, ldss)
    do_proj.tm_done = [0, 0]
    do_proj.se_bank = [0, 0]
    do_proj.acc_busy = 0
    for h in range(NH):
        do_proj(h, wq, h * 128, False, False, True, h)
    k.fence()
    if upto == "A2":
        return k

    Kg = [SD(0, BF16, 4096), SD(8192, BF16, 4096)]
    Vg = [SD(16384, BF16, 4096), SD(24576, BF16, 4096)]
    Qt = [SD(32768, BF16, 512), SD(33792, BF16, 512)]
    Pt = [SD(34816, BF16, 512), SD(35840, BF16, 512), SD(36864, BF16, 512),
          SD(53248, BF16, 512), SD(54272, BF16, 512)]
    rbS = SD(37888, F32, 512)
    ctxo = [SD(39936, BF16, 512), SD(40960, BF16, 512)]
    maskS = SD(43008, BF16, 8 * 512)
    recS = SD(51200, F32, 512, parts=1)

    mval = k.dma("sync", maskS.rearrange("p (a c) -> p a c", a=8),
                 masks.ap().rearrange("a p c -> p a c"), sQX)
    k.wait("vector", sQX, mval)
    k.wait("tensor", sCK, 1)
    k.wait("tensor", sCV, 1)
    k.wait("gpsimd", sCK, 1)
    k.wait("gpsimd", sCV, 1)

    kg_ld = [0, 0]
    kg_free = [0, 0]
    def load_g(g):
        buf = g % 2
        if kg_free[buf] > 0:
            k.wait("gpsimd", sTM, kg_free[buf])
        v = 0
        for r in range(2):
            for tcl in range(NTCH):
                v = k.dma("gpsimd", Kg[buf][:, (r * 4 + tcl) * 512:
                                            (r * 4 + tcl + 1) * 512],
                          kcc_o4[r, g, tcl], sQW)
            for q4 in range(4):
                dstv = Vg[buf][:, (r * 16 + q4 * 4) * 128:
                               (r * 16 + q4 * 4 + 4) * 128].rearrange(
                    "p (a d) -> p a d", a=4)
                srcv = vcc_o4[r, g, q4 * 4:(q4 + 1) * 4].rearrange(
                    "a p d -> p a d")
                v = k.dma("gpsimd", dstv, srcv, sQW)
        kg_ld[buf] = v

    load_g(0)
    qt_ld = [0, 0]
    qt_free = [0, 0]
    se_p = [0] * 5
    ve_p = [0] * 5
    tm_p = [0] * 5
    se_sbank = [0, 0]
    ve_ctx = [0, 0]
    qx_ctx = [0, 0]
    kcount = 0
    lnT = SD(57344, F32, 512, parts=1)

    def qt_load(it):
        h_, tc_ = (it // NTCH), it % NTCH
        qb_ = it % 2
        if qt_free[qb_] > 0:
            k.wait("sync", sTM, qt_free[qb_])
        qt_ld[qb_] = k.dma("sync", Qt[qb_], qscr[h_, tc_], sQX)

    ep_state = {
        "ep_prev": None,
        "tm_sum_prev": 0,
        "se_ln_prev": 0,
        "se_rbs_prev": 0,
        "ve_ctxev_prev": 0,
        "tm_bc_prev": 0,
    }

    def ep_head(it, tm_sum):
        k.wait("scalar", sTM, tm_sum)
        if ep_state["tm_bc_prev"] > 0:
            k.wait("scalar", sTM, ep_state["tm_bc_prev"])
        k.em("scalar", "activation", lnT, p7[0:1, :], AF.Ln, inc=[(sSE, 1)])
        ep_state["se_ln_prev"] = sSE.inc()
        k.wait("scalar", sSE)
        k.em("scalar", "activation", recS, lnT, AF.Exp, scale=-1.0,
             inc=[(sSE, 1)])
        return {"it": it, "se_exp": sSE.inc()}

    def ep_bcast(ep):
        k.wait("tensor", sSE, ep["se_exp"])
        if ep_state["se_rbs_prev"] > 0:
            k.wait("tensor", sSE, ep_state["se_rbs_prev"])
        k.em("tensor", "matmul", p6[:], onfr, recS,
             start=True, stop=True, inc=[(sTM, 1)])
        ep["tm_bc"] = sTM.inc()
        ep_state["tm_bc_prev"] = ep["tm_bc"]

    def ep_rbs(ep):
        k.wait("scalar", sTM, ep["tm_bc"])
        if ep_state["ve_ctxev_prev"] > 0:
            k.wait("scalar", sVE, ep_state["ve_ctxev_prev"])
        k.em("scalar", "activation", rbS, p6[:], AF.Copy, inc=[(sSE, 1)])
        ep["se_rbs"] = sSE.inc()
        ep_state["se_rbs_prev"] = ep["se_rbs"]

    def ep_ctxev(ep):
        it = ep["it"]
        k.wait("vector", sSE, ep["se_rbs"])
        if qx_ctx[it % 2] > 0:
            k.wait("vector", sQX, qx_ctx[it % 2])
        k.em("vector", "tensor_tensor", ctxo[it % 2], p[2 + it % 2][:], rbS,
             ALU.mult, inc=[(sVE, 1)])
        ve_ctx[it % 2] = sVE.inc()
        ep_state["ve_ctxev_prev"] = ve_ctx[it % 2]
        k.wait("sync", sVE, ve_ctx[it % 2])
        qx_ctx[it % 2] = k.dma("sync", ctxscr[it // NTCH, it % NTCH],
                               ctxo[it % 2], sQX)

    qt_load(0)
    for g in range(NKV):
        if g + 1 < NKV:
            load_g(g + 1)
        buf = g % 2
        k.wait("tensor", sQW, kg_ld[buf])
        for hh in range(4):
            h = 4 * g + hh
            for tc in range(NTCH):
                it = h * NTCH + tc
                qb = it % 2
                qv = qt_ld[qb]
                if it + 1 < NH * NTCH:
                    qt_load(it + 1)
                nkb = 8 * tc + 8
                cbank = p[2 + (it % 2)]
                k.wait("tensor", sQX, qv)

                def s_mm(kb):
                    sb_ = kb % 2
                    if se_sbank[sb_] > 0:
                        k.wait("tensor", sSE, se_sbank[sb_])
                    k.em("tensor", "matmul", p[sb_][:],
                         Kg[buf][:, kb * 128:(kb + 1) * 128], Qt[qb],
                         start=True, stop=True, inc=[(sTM, 1)])
                    v = sTM.inc()
                    if kb == nkb - 1:
                        qt_free[qb] = v
                    return v

                mm_s = [0] * nkb
                mm_s[0] = s_mm(0)
                for kb in range(nkb):
                    sl = kcount % 5
                    kcount += 1
                    sb_ = (kb % 2)
                    sbank = p[sb_]
                    if kb + 1 < nkb:
                        mm_s[kb + 1] = s_mm(kb + 1)
                    if kb == 1 and ep_state["ep_prev"] is not None:
                        ep_bcast(ep_state["ep_prev"])
                    if kb == 2 and ep_state["ep_prev"] is not None:
                        ep_rbs(ep_state["ep_prev"])
                        ep_ctxev(ep_state["ep_prev"])
                        ep_state["ep_prev"] = None
                    k.wait("scalar", sTM, mm_s[kb])
                    if tm_p[sl] > 0:
                        k.wait("scalar", sTM, tm_p[sl])
                    if ve_p[sl] > 0:
                        k.wait("scalar", sVE, ve_p[sl])
                    k.em("scalar", "activation", Pt[sl], sbank[:], AF.Exp,
                         scale=SCALE, inc=[(sSE, 1)])
                    se_p[sl] = sSE.inc()
                    se_sbank[sb_] = se_p[sl]
                    r = kb - 8 * tc
                    if r >= 0:
                        k.wait("vector", sSE, se_p[sl])
                        k.em("vector", "tensor_tensor", Pt[sl], Pt[sl],
                             maskS[:, r * 512:(r + 1) * 512], ALU.mult,
                             inc=[(sVE, 1)])
                        ve_p[sl] = sVE.inc()
                        k.wait("tensor", sVE, ve_p[sl])
                    else:
                        k.wait("tensor", sSE, se_p[sl])
                    if kb == 0 and ve_ctx[it % 2] > 0:
                        k.wait("tensor", sVE, ve_ctx[it % 2])
                    if kb == 0 and ep_state["se_ln_prev"] > 0:
                        k.wait("tensor", sSE, ep_state["se_ln_prev"])
                    k.em("tensor", "matmul", cbank[:],
                         Vg[buf][:, kb * 128:(kb + 1) * 128], Pt[sl],
                         start=(kb == 0), stop=(kb == nkb - 1),
                         skip_group_check=True)
                    k.em("tensor", "matmul", p7[0:1, :], onb, Pt[sl],
                         start=(kb == 0), stop=(kb == nkb - 1),
                         skip_group_check=True, inc=[(sTM, 1)])
                    tm_p[sl] = sTM.inc()
                if tc == NTCH - 1 and hh == 3:
                    kg_free[buf] = tm_p[(kcount - 1) % 5]
                ep_state["ep_prev"] = ep_head(it, tm_p[(kcount - 1) % 5])
    ep_bcast(ep_state["ep_prev"])
    ep_rbs(ep_state["ep_prev"])
    ep_ctxev(ep_state["ep_prev"])
    k.fence()
    if upto == "B":
        return k

    wos = [SD(0, BF16, 32 * 128), SD(8192, BF16, 32 * 128)]
    scb = [SD(16384, BF16, 512), SD(17408, BF16, 512)]
    gw1s = SD(18432, BF16, 4096, parts=32)
    gw2s = SD(26624, BF16, 32, parts=8)
    gvecS = SD(26752, F32, 104, parts=1)

    for h in range(NH):
        k.dma("sync",
              ctxT[:, h * 2048:(h + 1) * 2048].rearrange(
                  "p (a t) -> p a t", a=4),
              ctxscr[h].rearrange("a p t -> p a t"), sQX)
    ld_ctx = sQX.n

    ldg1 = k.dma("gpsimd", gw1s, gw1.ap(), sQW)
    ldg2 = k.dma("gpsimd", gw2s, gw2.ap(), sQW)
    ldgv = k.dma("gpsimd", gvecS, gvec.ap(), sQW)
    k.wait("tensor", sQW, ldg1)
    for grp in range(4):
        half = grp % 2
        k.wait("tensor", sVE)
        for q in range(8):
            i = grp * 8 + q
            k.em("tensor", "transpose",
                 p5[0:128, half * 512 + q * 32: half * 512 + (q + 1) * 32],
                 gw1s[:, i * 128:(i + 1) * 128], idb[0:32, 0:32],
                 inc=[(sTM, 1)] if q == 7 else None)
        tg = sTM.inc()
        k.wait("vector", sTM, tg)
        dst3 = gw1T.ap().rearrange("p (a c) -> p a c", c=32)[
            :, grp * 8:(grp + 1) * 8, :]
        src3 = p5[:, half * 512: half * 512 + 256].rearrange(
            "p (a c) -> p a c", a=8)
        k.em("vector", "tensor_copy", dst3, src3, inc=[(sVE, 1)])
        sVE.inc()
    k.wait("tensor", sQW, ldg2)
    k.wait("tensor", sVE)
    k.em("tensor", "transpose", p5[0:32, 0:8], gw2s, idb[0:8, 0:8],
         inc=[(sTM, 1)])
    tg2 = sTM.inc()
    k.wait("vector", sTM, tg2)
    k.em("vector", "tensor_copy", gw2T, p5[0:32, 0:8], inc=[(sVE, 1)])
    sVE.inc()
    k.wait("tensor", sQW, ldgv)
    k.em("tensor", "matmul", p6[:, 0:104], onfr, gvecS, start=True, stop=True,
         inc=[(sTM, 1)])
    tgv = sTM.inc()
    k.wait("vector", sTM, tgv)
    k.em("vector", "tensor_copy", gvb, p6[:, 0:104], inc=[(sVE, 1)])
    sVE.inc()

    if upto == "Cp":
        return k
    wo_tm = [0, 0]
    wo_se = [0, 0]
    wo_qs = [0, 0]
    nob_c = 2 if upto == "Cq" else 32
    nob_c = int(os.environ.get("TRNC_OBS", nob_c))
    ser_c = os.environ.get("TRNC_SERIAL", "") == "1"
    for ob in range(nob_c):
        buf = ob % 2
        if wo_tm[buf] > 0:
            k.wait("gpsimd", sTM, wo_tm[buf])
        src = wo.ap()[:, ob * 128:(ob + 1) * 128].rearrange(
            "(a p) c -> p a c", p=128)
        ldw = k.dma("gpsimd", wos[buf][:, :].rearrange("p (a c) -> p a c",
                                                       c=128), src, sQW)
        for tc in range(NTCH):
            it = ob * NTCH + tc
            bank = p[it % 2]
            if ser_c:
                k.fence()
            if wo_se[it % 2] > 0:
                k.wait("tensor", sSE, wo_se[it % 2])
            k.wait("tensor", sQW, ldw)
            k.wait("tensor", sQX, ld_ctx)
            for dv in range(32):
                k.em("tensor", "matmul", bank[:],
                     wos[buf][:, dv * 128:(dv + 1) * 128],
                     ctxT[:, dv * 2048 + tc * 512: dv * 2048 + tc * 512 + 512],
                     start=(dv == 0), stop=(dv == 31),
                     inc=[(sTM, 1)] if dv == 31 else None)
            mm = sTM.inc()
            if tc == NTCH - 1:
                wo_tm[buf] = mm
            k.wait("scalar", sTM, mm)
            if wo_qs[it % 2] > 0:
                k.wait("scalar", sQS, wo_qs[it % 2])
            k.em("scalar", "activation", scb[it % 2], bank[:], AF.Copy,
                 inc=[(sSE, 1)])
            wo_se[it % 2] = sSE.inc()
            k.wait("scalar", sSE, wo_se[it % 2])
            wo_qs[it % 2] = k.dma("scalar", aoscr[ob, tc], scb[it % 2], sQS)
    k.fence()
    if upto in ("C", "Cq"):
        return k

    aost = SD(0, BF16, 4096)
    aotm = SD(8192, BF16, 4096)
    x0f2 = SD(16384, F32, 4096)
    x1f = SD(32768, F32, 4096)
    xftm = SD(49152, BF16, 4096)
    hfb = SD(57344, F32, 32)
    hbb = SD(57600, BF16, 32)
    gfb = SD(57728, F32, 8)
    m1b = SD(57856, F32, 1)
    m2b = SD(57920, F32, 1)
    gml = SD(57984, F32, 8)
    sgd = SD(58048, F32, 1)
    wjf = SD(58112, F32, 8)
    wjf2 = SD(58176, F32, 8)
    wjb = SD(58240, BF16, 8)

    ve_done_prev = 0
    tm_tr_prev = 0
    qw_x1_prev = 0
    for j in range(NTB):
        tcj, sub = j // 4, j % 4
        if ve_done_prev > 0:
            k.wait("sync", sVE, ve_done_prev)
        lda = k.dma("sync",
                    aost.rearrange("p (a t) -> p a t", a=32),
                    aoscr[0:32, tcj, :, sub * 128:(sub + 1) * 128]
                    .rearrange("a p t -> p a t"), sQX)
        ldx = k.dma("sync", x0f2, xq.ap()[j * 128:(j + 1) * 128, :], sQX)
        k.wait("tensor", sQX, lda)
        for grp in range(8):
            half = grp % 2
            k.wait("tensor", sVE)
            for q in range(4):
                i = grp * 4 + q
                k.em("tensor", "transpose",
                     p5[:, half * 512 + q * 128: half * 512 + (q + 1) * 128],
                     aost[:, i * 128:(i + 1) * 128], idb,
                     inc=[(sTM, 1)] if q == 3 else None)
            ttr = sTM.inc()
            k.wait("vector", sTM, ttr)
            dst3 = aotm.rearrange("p (a t) -> p a t", a=32)[
                :, grp * 4:(grp + 1) * 4, :]
            src3 = p5[:, half * 512:(half + 1) * 512].rearrange(
                "p (a t) -> p a t", a=4)
            k.em("vector", "tensor_copy", dst3, src3, inc=[(sVE, 1)])
            sVE.inc()
        tm_tr_prev = sTM.n
        k.wait("scalar", sVE)
        k.em("scalar", "activation", xftm, aotm, AF.Square, accum_out=S2t,
             inc=[(sSE, 1)])
        se_sq = sSE.inc()
        k.em("vector", "reduce_sum", S1t, aotm, axis=AXX, inc=[(sVE, 1)])
        sVE.inc()
        k.wait("vector", sVE)
        k.em("vector", "tensor_scalar", Mt, S1t, INV_D, None, ALU.mult,
             inc=[(sVE, 1)])
        sVE.inc()
        k.wait("vector", sVE)
        k.em("vector", "tensor_scalar", MSQt, Mt, Mt, None, ALU.mult,
             inc=[(sVE, 1)])
        sVE.inc()
        k.wait("vector", sSE, se_sq)
        k.em("vector", "tensor_scalar", VARt, S2t, INV_D, None, ALU.mult)
        k.wait("vector", sVE)
        k.em("vector", "tensor_tensor", VARt, VARt, MSQt, ALU.subtract,
             inc=[(sVE, 1)])
        sVE.inc()
        k.wait("scalar", sVE)
        k.em("scalar", "activation", SQt, VARt, AF.Sqrt, bias=epsS,
             inc=[(sSE, 1)])
        sSE.inc()
        k.wait("vector", sSE)
        k.em("vector", "reciprocal", RSt, SQt, inc=[(sVE, 1)])
        sVE.inc()
        k.wait("vector", sVE)
        if qw_x1_prev > 0:
            k.wait("vector", sQW, qw_x1_prev)
        k.em("vector", "tensor_scalar", x1f, aotm, Mt, RSt, ALU.subtract,
             ALU.mult, inc=[(sVE, 1)])
        sVE.inc()
        k.wait("vector", sQX, ldx)
        k.wait("vector", sVE)
        k.em("vector", "tensor_tensor", x1f, x1f, x0f2, ALU.add,
             inc=[(sVE, 1)])
        ve_x1 = sVE.inc()
        k.wait("gpsimd", sVE, ve_x1)
        qw_x1_prev = k.dma("gpsimd", x1scr[j], x1f, sQW)
        k.wait("scalar", sVE, ve_x1)
        k.em("scalar", "activation", aotm, x1f, AF.Square, accum_out=S2t,
             inc=[(sSE, 1)])
        se_sq2 = sSE.inc()
        k.em("vector", "reduce_sum", S1t, x1f, axis=AXX, inc=[(sVE, 1)])
        sVE.inc()
        k.wait("vector", sVE)
        k.em("vector", "tensor_scalar", Mt, S1t, INV_D, None, ALU.mult,
             inc=[(sVE, 1)])
        sVE.inc()
        k.wait("vector", sVE)
        k.em("vector", "tensor_scalar", MSQt, Mt, Mt, None, ALU.mult,
             inc=[(sVE, 1)])
        sVE.inc()
        k.wait("vector", sSE, se_sq2)
        k.em("vector", "tensor_scalar", VARt, S2t, INV_D, None, ALU.mult)
        k.wait("vector", sVE)
        k.em("vector", "tensor_tensor", VARt, VARt, MSQt, ALU.subtract,
             inc=[(sVE, 1)])
        sVE.inc()
        k.wait("scalar", sVE)
        k.em("scalar", "activation", SQt, VARt, AF.Sqrt, bias=epsS,
             inc=[(sSE, 1)])
        sSE.inc()
        k.wait("vector", sSE)
        k.em("vector", "reciprocal", RSt, SQt, inc=[(sVE, 1)])
        sVE.inc()
        k.wait("vector", sVE)
        k.em("vector", "tensor_scalar", xftm, x1f, Mt, RSt, ALU.subtract,
             ALU.mult, inc=[(sVE, 1)])
        ve_xf = sVE.inc()
        ve_done_prev = ve_xf
        for grp in range(8):
            half = grp % 2
            k.wait("tensor", sVE)
            k.wait("tensor", sVE, ve_xf)
            for q in range(4):
                i = grp * 4 + q
                k.em("tensor", "transpose",
                     p5[:, half * 512 + q * 128: half * 512 + (q + 1) * 128],
                     xftm[:, i * 128:(i + 1) * 128], idb,
                     inc=[(sTM, 1)] if q == 3 else None)
            ttr = sTM.inc()
            k.wait("vector", sTM, ttr)
            dst3 = xfT.rearrange("p (a t) -> p a t", t=2048)[
                :, grp * 4:(grp + 1) * 4, j * 128:(j + 1) * 128]
            src3 = p5[:, half * 512:(half + 1) * 512].rearrange(
                "p (a t) -> p a t", a=4)
            k.em("vector", "tensor_copy", dst3, src3, inc=[(sVE, 1)])
            sVE.inc()
        ve_xfT = sVE.n
        k.wait("tensor", sVE, ve_xfT)
        for i in range(32):
            k.em("tensor", "matmul", p6[:, 0:32],
                 xfT.rearrange("p (a t) -> p a t", t=2048)[
                     :, i, j * 128:(j + 1) * 128],
                 gw1T.ap().rearrange("p (a c) -> p a c", c=32)[:, i, :],
                 start=(i == 0), stop=(i == 31),
                 inc=[(sTM, 1)] if i == 31 else None)
        mm_h = sTM.inc()
        k.wait("vector", sTM, mm_h)
        k.em("vector", "tensor_tensor", hfb, p6[:, 0:32], gvb[:, 0:32],
             ALU.add, inc=[(sVE, 1)])
        sVE.inc()
        k.wait("vector", sVE)
        k.em("vector", "reduce_sum", S1t, hfb, axis=AXX, inc=[(sVE, 1)])
        sVE.inc()
        k.wait("vector", sVE)
        k.em("vector", "tensor_scalar", Mt, S1t, 1.0 / 32.0, None, ALU.mult,
             inc=[(sVE, 1)])
        sVE.inc()
        k.wait("vector", sVE)
        k.em("vector", "tensor_scalar", MSQt, Mt, Mt, None, ALU.mult,
             inc=[(sVE, 1)])
        sVE.inc()
        k.wait("scalar", sVE)
        k.em("scalar", "activation", hbb, hfb, AF.Square, accum_out=S2t,
             inc=[(sSE, 1)])
        sSE.inc()
        k.wait("vector", sSE)
        k.em("vector", "tensor_scalar", VARt, S2t, 1.0 / 32.0, None, ALU.mult)
        k.wait("vector", sVE)
        k.em("vector", "tensor_tensor", VARt, VARt, MSQt, ALU.subtract,
             inc=[(sVE, 1)])
        sVE.inc()
        k.wait("scalar", sVE)
        k.em("scalar", "activation", SQt, VARt, AF.Sqrt, bias=epsS,
             inc=[(sSE, 1)])
        sSE.inc()
        k.wait("vector", sSE)
        k.em("vector", "reciprocal", RSt, SQt, inc=[(sVE, 1)])
        sVE.inc()
        k.wait("vector", sVE)
        k.em("vector", "tensor_scalar", hfb, hfb, Mt, RSt, ALU.subtract,
             ALU.mult, inc=[(sVE, 1)])
        sVE.inc()
        k.wait("vector", sVE)
        k.em("vector", "tensor_tensor", hfb, hfb, gvb[:, 32:64], ALU.mult,
             inc=[(sVE, 1)])
        sVE.inc()
        k.wait("vector", sVE)
        k.em("vector", "tensor_tensor", hfb, hfb, gvb[:, 64:96], ALU.add,
             inc=[(sVE, 1)])
        sVE.inc()
        k.wait("vector", sVE)
        k.em("vector", "tensor_scalar", hbb, hfb, 0.0, None, ALU.max,
             inc=[(sVE, 1)])
        ve_h = sVE.inc()
        k.wait("tensor", sVE, ve_h)
        k.em("tensor", "transpose", p5[0:32, 512:640], hbb, idb,
             inc=[(sTM, 1)])
        tm_ht = sTM.inc()
        k.wait("vector", sTM, tm_ht)
        k.em("vector", "tensor_copy", wTall.ap()[0:32, 0:128],
             p5[0:32, 512:640], inc=[(sVE, 1)])
        ve_ht = sVE.inc()
        k.wait("tensor", sVE, ve_ht)
        k.em("tensor", "matmul", p6[:, 64:72], wTall.ap()[0:32, 0:128],
             gw2T, start=True, stop=True, inc=[(sTM, 1)])
        mm_g = sTM.inc()
        k.wait("vector", sTM, mm_g)
        k.em("vector", "tensor_tensor", gfb, p6[:, 64:72], gvb[:, 96:104],
             ALU.add, inc=[(sVE, 1)])
        sVE.inc()
        k.wait("vector", sVE)
        k.em("vector", "reduce_max", m1b, gfb, axis=AXX, inc=[(sVE, 1)])
        sVE.inc()
        k.wait("vector", sVE)
        k.em("vector", "tensor_scalar", issb.ap()[:, j * 16: j * 16 + 8],
             gfb, m1b, None, ALU.is_equal, inc=[(sVE, 1)])
        sVE.inc()
        k.wait("vector", sVE)
        k.em("vector", "tensor_scalar", gml,
             issb.ap()[:, j * 16: j * 16 + 8], -1e30, None, ALU.mult,
             inc=[(sVE, 1)])
        sVE.inc()
        k.wait("vector", sVE)
        k.em("vector", "tensor_tensor", gml, gml, gfb, ALU.add,
             inc=[(sVE, 1)])
        sVE.inc()
        k.wait("vector", sVE)
        k.em("vector", "reduce_max", m2b, gml, axis=AXX, inc=[(sVE, 1)])
        sVE.inc()
        k.wait("vector", sVE)
        k.em("vector", "tensor_scalar", issb.ap()[:, j * 16 + 8: j * 16 + 16],
             gml, m2b, None, ALU.is_equal, inc=[(sVE, 1)])
        sVE.inc()
        k.wait("vector", sVE)
        k.em("vector", "tensor_tensor", sgd, m1b, m2b, ALU.subtract,
             inc=[(sVE, 1)])
        ve_sgd = sVE.inc()
        k.wait("scalar", sVE, ve_sgd)
        k.em("scalar", "activation", g12.ap()[:, 2 * j: 2 * j + 1], sgd,
             AF.Sigmoid, inc=[(sSE, 1)])
        se_g1 = sSE.inc()
        k.wait("vector", sSE, se_g1)
        k.em("vector", "tensor_scalar", g12.ap()[:, 2 * j + 1: 2 * j + 2],
             g12.ap()[:, 2 * j: 2 * j + 1], -1.0, 1.0, ALU.mult, ALU.add,
             inc=[(sVE, 1)])
        sVE.inc()
        k.wait("tensor", sVE)
        k.em("tensor", "matmul", p7[0:1, 0:16], onb,
             issb.ap()[:, j * 16:(j + 1) * 16],
             start=(j == 0), stop=(j == NTB - 1), skip_group_check=True,
             inc=[(sTM, 1)])
        sTM.inc()
    k.wait("vector", sTM)
    k.em("vector", "tensor_copy", cntS, p7[0:1, 0:16], inc=[(sVE, 1)])
    ve_cnt = sVE.inc()
    k.wait("gpsimd", sVE, ve_cnt)
    k.dma("gpsimd", cnt_in.ap(), cntS, sQW)
    k.wait("gpsimd", sQW)
    k.em("gpsimd", "collective_compute", "AllReduce", ALU.add,
         [[0, 1, 2, 3, 4, 5, 6, 7]],
         [cnt_in.ap().opt()], [cnt_out.ap().opt()], inc=[(sCN, 1)])
    sCN.inc()
    k.fence()
    if upto == "C2":
        return k

    asb = SD(0, BF16, 4096)
    bsb = SD(8192, BF16, 4096)
    tmpT = SD(16384, F32, 2048)
    scT = SD(24576, BF16, 2048)
    weT = SD(28672, BF16, 2048)
    wband = [SD(32768, BF16, 4096), SD(40960, BF16, 4096)]
    WTb = [SD(16384, BF16, 4096), SD(0, BF16, 4096)]
    fob = [SD(49152, BF16, 512), SD(50176, BF16, 512)]

    lda_ = k.dma("gpsimd", asb.rearrange("p (a c) -> p a c", c=128),
                 astack.ap().rearrange("(a p) c -> p a c", p=128), sQW)
    ldb_ = k.dma("gpsimd", bsb, bstack.ap(), sQW)
    for tc in range(NTCH):
        bank = p[tc % 2]
        k.wait("tensor", sQW, lda_)
        k.wait("tensor", sVE)
        for i in range(32):
            k.em("tensor", "matmul", bank[:],
                 asb.rearrange("p (a c) -> p a c", c=128)[:, i, :],
                 xfT.rearrange("p (a t) -> p a t", t=2048)[
                     :, i, tc * 512:(tc + 1) * 512],
                 start=(i == 0), stop=(i == 31),
                 inc=[(sTM, 1)] if i == 31 else None)
        mm = sTM.inc()
        k.wait("vector", sTM, mm)
        k.em("vector", "tensor_copy", tmpT[:, tc * 512:(tc + 1) * 512],
             bank[:], inc=[(sVE, 1)])
        sVE.inc()
    k.wait("sync", sCN, 1)
    ldc = k.dma("sync", cntS, cnt_out.ap(), sQX)
    k.wait("vector", sQX, ldc)
    k.em("vector", "tensor_scalar", alwS, cntS, CAPACITY, None, ALU.is_le,
         inc=[(sVE, 1)])
    ve_alw = sVE.inc()
    k.wait("tensor", sVE, ve_alw)
    k.em("tensor", "matmul", p6[:, 0:16], onfr, alwS, start=True, stop=True,
         inc=[(sTM, 1)])
    tm_alw = sTM.inc()
    k.wait("vector", sTM, tm_alw)
    k.em("vector", "tensor_copy", alwB, p6[:, 0:16], inc=[(sVE, 1)])
    sVE.inc()
    for j in range(NTB):
        k.wait("vector", sVE)
        k.em("vector", "tensor_scalar", wjf,
             issb.ap()[:, j * 16: j * 16 + 8],
             g12.ap()[:, 2 * j: 2 * j + 1], None, ALU.mult, inc=[(sVE, 1)])
        sVE.inc()
        k.wait("vector", sVE)
        k.em("vector", "tensor_tensor", wjf, wjf, alwB[:, 0:8], ALU.mult,
             inc=[(sVE, 1)])
        sVE.inc()
        k.em("vector", "tensor_scalar", wjf2,
             issb.ap()[:, j * 16 + 8: j * 16 + 16],
             g12.ap()[:, 2 * j + 1: 2 * j + 2], None, ALU.mult,
             inc=[(sVE, 1)])
        sVE.inc()
        k.wait("vector", sVE)
        k.em("vector", "tensor_tensor", wjf2, wjf2, alwB[:, 8:16], ALU.mult,
             inc=[(sVE, 1)])
        sVE.inc()
        k.wait("vector", sVE)
        k.em("vector", "tensor_tensor", wjb, wjf, wjf2, ALU.add,
             inc=[(sVE, 1)])
        ve_wj = sVE.inc()
        k.wait("tensor", sVE, ve_wj)
        k.em("tensor", "transpose", p5[0:8, 512:640], wjb, idb,
             inc=[(sTM, 1)])
        tm_wt = sTM.inc()
        k.wait("vector", sTM, tm_wt)
        k.em("vector", "tensor_copy",
             wTall.ap()[0:8, j * 128:(j + 1) * 128], p5[0:8, 512:640],
             inc=[(sVE, 1)])
        sVE.inc()
    ve_wtall = sVE.n
    for tc in range(NTCH):
        k.wait("tensor", sVE)
        k.em("tensor", "matmul", p6[:], erp,
             wTall.ap()[0:8, tc * 512:(tc + 1) * 512], start=True, stop=True,
             inc=[(sTM, 1)])
        tm_we = sTM.inc()
        k.wait("vector", sTM, tm_we)
        k.em("vector", "tensor_copy", weT[:, tc * 512:(tc + 1) * 512],
             p6[:], inc=[(sVE, 1)])
        sVE.inc()
    k.wait("vector", sVE)
    k.em("vector", "tensor_tensor", scT, tmpT, weT, ALU.mult, inc=[(sVE, 1)])
    ve_sc = sVE.inc()

    d3_tm = [0, 0]
    d3_wt = [0, 0]
    d3_se = [0, 0]
    d3_qs = [0, 0]
    for ob in range(32):
        buf = ob % 2
        if d3_tm[buf] > 0:
            k.wait("gpsimd", sTM, d3_tm[buf])
        ldw = k.dma("gpsimd", wband[buf],
                    wffn.ap()[ob * 128:(ob + 1) * 128, :], sQW)
        k.wait("tensor", sQW, ldw)
        if ob >= 2:
            k.wait("tensor", sVE, d3_wt[buf])
        for grp in range(8):
            half = grp % 2
            k.wait("tensor", sVE)
            for q in range(4):
                i = grp * 4 + q
                k.em("tensor", "transpose",
                     p5[:, half * 512 + q * 128: half * 512 + (q + 1) * 128],
                     wband[buf][:, i * 128:(i + 1) * 128], idb,
                     inc=[(sTM, 1)] if q == 3 else None)
            ttr = sTM.inc()
            k.wait("vector", sTM, ttr)
            dst3 = WTb[buf].rearrange("p (a c) -> p a c", c=128)[
                :, grp * 4:(grp + 1) * 4, :]
            src3 = p5[:, half * 512:(half + 1) * 512].rearrange(
                "p (a c) -> p a c", a=4)
            k.em("vector", "tensor_copy", dst3, src3, inc=[(sVE, 1)])
            sVE.inc()
        d3_wt[buf] = sVE.n
        k.wait("tensor", sVE, d3_wt[buf])
        if ob == 0:
            k.wait("tensor", sVE, ve_sc)
            k.wait("tensor", sQW, ldb_)
        for tc in range(NTCH):
            it = ob * NTCH + tc
            bank = p[2 + it % 2]
            if d3_se[it % 2] > 0:
                k.wait("tensor", sSE, d3_se[it % 2])
            for i in range(32):
                k.em("tensor", "matmul", bank[:],
                     WTb[buf].rearrange("p (a c) -> p a c", c=128)[:, i, :],
                     xfT.rearrange("p (a t) -> p a t", t=2048)[
                         :, i, tc * 512:(tc + 1) * 512],
                     start=(i == 0), stop=False)
            k.em("tensor", "matmul", bank[:],
                 bsb[:, ob * 128:(ob + 1) * 128],
                 scT[:, tc * 512:(tc + 1) * 512],
                 start=False, stop=True, skip_group_check=True,
                 inc=[(sTM, 1)])
            mm = sTM.inc()
            if tc == NTCH - 1:
                d3_tm[buf] = mm
            k.wait("scalar", sTM, mm)
            if d3_qs[it % 2] > 0:
                k.wait("scalar", sQS, d3_qs[it % 2])
            k.em("scalar", "activation", fob[it % 2], bank[:], AF.Copy,
                 inc=[(sSE, 1)])
            d3_se[it % 2] = sSE.inc()
            k.wait("scalar", sSE, d3_se[it % 2])
            d3_qs[it % 2] = k.dma("scalar", foscr[ob, tc], fob[it % 2], sQS)
    k.fence()
    if upto == "D":
        return k

    fost = SD(0, BF16, 4096)
    fotm = SD(8192, BF16, 4096)
    x1b = [SD(16384, BF16, 4096), SD(24576, BF16, 4096)]
    outb = [SD(32768, F32, 4096), SD(49152, F32, 4096)]

    ve_out = [0, 0]
    qx_out = [0, 0]
    ve_prev = 0
    for j in range(NTB):
        tcj, sub = j // 4, j % 4
        if ve_prev > 0:
            k.wait("sync", sVE, ve_prev)
        ldf = k.dma("sync",
                    fost.rearrange("p (a t) -> p a t", a=32),
                    foscr[0:32, tcj, :, sub * 128:(sub + 1) * 128]
                    .rearrange("a p t -> p a t"), sQX)
        ldx1 = k.dma("sync", x1b[j % 2], x1scr[j], sQX)
        k.wait("tensor", sQX, ldf)
        for grp in range(8):
            half = grp % 2
            k.wait("tensor", sVE)
            for q in range(4):
                i = grp * 4 + q
                k.em("tensor", "transpose",
                     p5[:, half * 512 + q * 128: half * 512 + (q + 1) * 128],
                     fost[:, i * 128:(i + 1) * 128], idb,
                     inc=[(sTM, 1)] if q == 3 else None)
            ttr = sTM.inc()
            k.wait("vector", sTM, ttr)
            dst3 = fotm.rearrange("p (a t) -> p a t", a=32)[
                :, grp * 4:(grp + 1) * 4, :]
            src3 = p5[:, half * 512:(half + 1) * 512].rearrange(
                "p (a t) -> p a t", a=4)
            k.em("vector", "tensor_copy", dst3, src3, inc=[(sVE, 1)])
            sVE.inc()
        k.wait("scalar", sVE)
        k.em("scalar", "activation", fost, fotm, AF.Square, accum_out=S2t,
             inc=[(sSE, 1)])
        se_sq = sSE.inc()
        k.em("vector", "reduce_sum", S1t, fotm, axis=AXX, inc=[(sVE, 1)])
        sVE.inc()
        k.wait("vector", sVE)
        k.em("vector", "tensor_scalar", Mt, S1t, INV_D, None, ALU.mult,
             inc=[(sVE, 1)])
        sVE.inc()
        k.wait("vector", sVE)
        k.em("vector", "tensor_scalar", MSQt, Mt, Mt, None, ALU.mult,
             inc=[(sVE, 1)])
        sVE.inc()
        k.wait("vector", sSE, se_sq)
        k.em("vector", "tensor_scalar", VARt, S2t, INV_D, None, ALU.mult)
        k.wait("vector", sVE)
        k.em("vector", "tensor_tensor", VARt, VARt, MSQt, ALU.subtract,
             inc=[(sVE, 1)])
        sVE.inc()
        k.wait("scalar", sVE)
        k.em("scalar", "activation", SQt, VARt, AF.Sqrt, bias=epsS,
             inc=[(sSE, 1)])
        sSE.inc()
        k.wait("vector", sSE)
        k.em("vector", "reciprocal", RSt, SQt, inc=[(sVE, 1)])
        sVE.inc()
        ob = outb[j % 2]
        if qx_out[j % 2] > 0:
            k.wait("vector", sQX, qx_out[j % 2])
        k.wait("vector", sVE)
        k.em("vector", "tensor_scalar", ob, fotm, Mt, RSt, ALU.subtract,
             ALU.mult, inc=[(sVE, 1)])
        sVE.inc()
        k.wait("vector", sQX, ldx1)
        k.wait("vector", sVE)
        k.em("vector", "tensor_tensor", ob, ob, x1b[j % 2], ALU.add,
             inc=[(sVE, 1)])
        ve_out[j % 2] = sVE.inc()
        ve_prev = ve_out[j % 2]
        k.wait("sync", sVE, ve_out[j % 2])
        qx_out[j % 2] = k.dma("sync", out_d.ap()[j * 128:(j + 1) * 128, :],
                              ob, sQX)
    k.fence()

    return k


def _host_constants(s):
    bf = ml_dtypes.bfloat16
    qblocks = np.array([2 * i + s for i in range(NTB)])
    qpos = (qblocks[:, None] * 128 + np.arange(128)[None, :]).reshape(-1)
    kpos = np.arange(TPC) + 2048 * s
    inv = 1.0 / (500000.0 ** (np.arange(0, 128, 2, dtype=np.float64) / 128.0))

    def tabs(pos):
        fr = pos[:, None].astype(np.float64) * inv[None, :]
        c = np.cos(fr).T.astype(np.float32)
        sn = np.sin(fr).T.astype(np.float32)
        cx = np.vstack([c, c]).astype(bf)
        ss = np.vstack([-sn, sn]).astype(bf)
        return cx, ss

    cxq, ssq = tabs(qpos)
    cxk, ssk = tabs(kpos)
    kap = np.arange(128)[:, None]
    masks = np.zeros((8, 128, 512), dtype=np.float32)
    for r in range(8):
        for j in range(4):
            tau = np.arange(128)[None, :]
            valid = (128 * r + kap) <= (128 * (2 * j + s) + tau)
            masks[r, :, j * 128:(j + 1) * 128] = valid
    return {
        "cxq": np.ascontiguousarray(cxq), "ssq": np.ascontiguousarray(ssq),
        "cxk": np.ascontiguousarray(cxk), "ssk": np.ascontiguousarray(ssk),
        "masks": masks.astype(bf),
        "identb": np.eye(128, dtype=bf),
        "identf": np.eye(128, dtype=np.float32),
        "onesb": np.ones((128, 1), dtype=bf),
        "onesfr": np.ones((1, 128), dtype=np.float32),
        "epsd": np.full((128, 1), 1e-5, dtype=np.float32),
        "erep": (np.repeat(np.eye(8, dtype=np.float32), 16, axis=1)
                 / 16.0).astype(bf),
    }, qblocks


def kernel(L0_attn_x, L0_attn_Wq, L0_attn_Wk, L0_attn_Wv, L0_attn_Wo,
           L0_ffn_weight, L0_ffn_lora_A, L0_ffn_lora_B, L0_ffn_gw1,
           L0_ffn_gb1, L0_ffn_ln_gamma, L0_ffn_ln_beta, L0_ffn_gw2,
           L0_ffn_gb2):
    x = np.ascontiguousarray(np.asarray(L0_attn_x, dtype=np.float32))
    wq_ = np.ascontiguousarray(np.asarray(L0_attn_Wq, dtype=np.float32))
    wk_ = np.ascontiguousarray(np.asarray(L0_attn_Wk, dtype=np.float32))
    wv_ = np.ascontiguousarray(np.asarray(L0_attn_Wv, dtype=np.float32))
    wo_ = np.ascontiguousarray(np.asarray(L0_attn_Wo, dtype=np.float32))
    wf_ = np.ascontiguousarray(np.asarray(L0_ffn_weight, dtype=np.float32))
    astack = np.ascontiguousarray(
        np.asarray(L0_ffn_lora_A, dtype=np.float32).transpose(1, 0, 2)
        .reshape(D, 128))
    bstack = np.ascontiguousarray(
        np.asarray(L0_ffn_lora_B, dtype=np.float32).reshape(128, D))
    gw1_ = np.ascontiguousarray(np.asarray(L0_ffn_gw1, dtype=np.float32))
    gw2_ = np.ascontiguousarray(np.asarray(L0_ffn_gw2, dtype=np.float32))
    gvec = np.concatenate([
        np.asarray(L0_ffn_gb1, dtype=np.float32).ravel(),
        np.asarray(L0_ffn_ln_gamma, dtype=np.float32).ravel(),
        np.asarray(L0_ffn_ln_beta, dtype=np.float32).ravel(),
        np.asarray(L0_ffn_gb2, dtype=np.float32).ravel()])[None, :]
    gvec = np.ascontiguousarray(gvec)

    kb = build_program()
    nc = kb.finalize()

    in_maps = []
    qb_by_s = {}
    for c in range(NCORES):
        b, s = c // 2, c % 2
        consts, qblocks = _host_constants(s)
        qb_by_s[s] = qblocks
        xb = x[b].reshape(32, 128, D)
        m = {
            "xq": np.ascontiguousarray(xb[qblocks].reshape(TPC, D)),
            "xkv": np.ascontiguousarray(x[b, 2048 * s: 2048 * s + TPC]),
            "wq": wq_, "wk": wk_, "wv": wv_, "wo": wo_, "wffn": wf_,
            "astack": astack, "bstack": bstack,
            "gw1": gw1_, "gw2": gw2_, "gvec": gvec,
        }
        m.update(consts)
        in_maps.append(m)

    res = run_bass_kernel_spmd(nc, in_maps, core_ids=list(range(NCORES)))

    out = np.empty((B, T, D), dtype=np.float32)
    for c in range(NCORES):
        b, s = c // 2, c % 2
        shard = res.results[c]["out"].reshape(NTB, 128, D)
        ob = out[b].reshape(32, 128, D)
        ob[qb_by_s[s]] = shard
    return out


# version 08
# speedup vs baseline: 1.0023x; 1.0023x over previous
import os
import numpy as np
import ml_dtypes
import concourse.bass as bass
import concourse.mybir as mybir
from concourse.bass_utils import run_bass_kernel_spmd

F32 = mybir.dt.float32
BF16 = mybir.dt.bfloat16
U8 = mybir.dt.uint8
AF = mybir.ActivationFunctionType
ALU = mybir.AluOpType
AXX = mybir.AxisListType.X

B, T, D = 4, 4096, 4096
NCORES = 8
TPC = 2048
NH, NKV, HD = 32, 8, 128
NTB, NTCH = 16, 4
SCALE = 1.0 / float(np.sqrt(128.0))
EPS = 1e-5
CAPACITY = 6144.0
INV_D = 1.0 / 4096.0
ENGINES = ("sync", "tensor", "vector", "scalar", "gpsimd")


class S:
    __slots__ = ("h", "n")

    def __init__(self, h):
        self.h = h
        self.n = 0

    def inc(self, k=1):
        self.n += k
        return self.n


class KB:
    def __init__(self, dbg=()):
        self.nc = bass.Bass("TRN2", target_bir_lowering=False, debug=False)
        self.ops = {e: [] for e in ENGINES}
        self.dbg = set(dbg)

    def em(self, e, meth, *args, inc=None, **kw):
        self.ops[e].append((meth, args, kw, inc or []))

    def wait(self, e, s, v=None):
        self.em(e, "wait_ge", s.h, s.n if v is None else v)

    def dma(self, e, out, in_, s):
        self.em(e, "dma_start", inc=[(s, 16)], out=out, in_=in_)
        return s.inc(16)

    def fence(self):
        self.wait("sync", self.sQX)
        self.wait("gpsimd", self.sQW)
        self.wait("scalar", self.sQS)
        tgt = self.sFE.n + len(ENGINES)
        for e in ENGINES:
            self.em(e, "nop", inc=[(self.sFE, 1)])
        self.sFE.n = tgt
        for e in ENGINES:
            self.wait(e, self.sFE, tgt)

    def finalize(self):
        ops = self.ops

        with self.nc.Block() as block:
            def run_for(name):
                def body(eng):
                    for meth, args, kw, incs in ops[name]:
                        ins = getattr(eng, meth)(*args, **kw)
                        for (sm, v) in incs:
                            ins = ins.then_inc(sm.h, v)
                return body

            block.sync(run_for("sync"))
            block.tensor(run_for("tensor"))
            block.vector(run_for("vector"))
            block.scalar(run_for("scalar"))
            block.gpsimd(run_for("gpsimd"))
        return self.nc

    def sb(self, name, shape, dtype):
        return self.nc.alloc_sbuf_tensor(name, shape, dtype)

    def carve(self, region, off_bytes, dtype, cols, parts=128):
        esz = 2 if dtype == BF16 else 4
        assert off_bytes % esz == 0
        h = region.bitcast(dtype)
        ap = h.ap()[0:parts, off_bytes // esz: off_bytes // esz + cols]
        return ap

    def tap(self, name, ap, shape, dtype):
        if name not in self.dbg:
            return
        t = self.nc.dram_tensor("dbg_" + name, shape, dtype, kind="ExternalOutput").ap()
        self.wait("sync", self.sVE)
        self.wait("sync", self.sSE)
        self.dma("sync", t, ap, self.sQX)


def build_program(dbg=(), upto="E"):
    k = KB(dbg)
    nc = k.nc

    def din(name, shape, dtype=F32):
        return nc.dram_tensor(name, shape, dtype, kind="ExternalInput")

    xq = din("xq", [TPC, D])
    xkv = din("xkv", [TPC, D])
    wq = din("wq", [D, D])
    wk = din("wk", [D, 1024])
    wv = din("wv", [D, 1024])
    wo = din("wo", [D, D])
    wffn = din("wffn", [D, D])
    astack = din("astack", [D, 128])
    bstack = din("bstack", [128, D])
    gw1 = din("gw1", [32, D])
    gw2 = din("gw2", [8, 32])
    gvec = din("gvec", [1, 104])
    cxq = din("cxq", [128, TPC], BF16)
    ssq = din("ssq", [128, TPC], BF16)
    cxk = din("cxk", [128, TPC], BF16)
    ssk = din("ssk", [128, TPC], BF16)
    masks = din("masks", [8, 128, 512], BF16)
    identb = din("identb", [128, 128], BF16)
    identf = din("identf", [128, 128], F32)
    onesb = din("onesb", [128, 1], BF16)
    onesfr = din("onesfr", [1, 128], F32)
    erep = din("erep", [8, 128], BF16)
    epsd = din("epsd", [128, 1], F32)
    out_d = nc.dram_tensor("out", [TPC, D], F32, kind="ExternalOutput")

    def dint(name, shape, dtype=BF16):
        return nc.dram_tensor(name, shape, dtype, kind="Internal")

    qscr = dint("qscr", [NH, NTCH, 128, 512])
    kcc_in = dint("kcc_in", [NKV * NTCH * 128, 512])
    kcc_out = dint("kcc_out", [2 * NKV * NTCH * 128, 512])
    vcc_in = dint("vcc_in", [NKV * 16 * 128, 128])
    vcc_out = dint("vcc_out", [2 * NKV * 16 * 128, 128])
    ctxscr = dint("ctxscr", [NH, NTCH, 128, 512])
    aoscr = dint("aoscr", [32, NTCH, 128, 512])
    x1scr = dint("x1scr", [NTB, 128, D])
    foscr = dint("foscr", [32, NTCH, 128, 512])
    cnt_in = dint("cnt_in", [1, 16], F32)
    cnt_out = dint("cnt_out", [1, 16], F32)

    kcc4 = kcc_in.reshape([NKV, NTCH, 128, 512])
    kcc_o4 = kcc_out.reshape([2, NKV, NTCH, 128, 512])
    vcc4 = vcc_in.reshape([NKV, 16, 128, 128])
    vcc_o4 = vcc_out.reshape([2, NKV, 16, 128, 128])

    for s_ in ("sQX", "sQW", "sQS", "sTM", "sVE", "sSE", "sFE", "sCK", "sCV", "sCN", "sPL"):
        setattr(k, s_, S(nc.alloc_semaphore(s_)))
    sQX, sQW, sQS = k.sQX, k.sQW, k.sQS
    sTM, sVE, sSE = k.sTM, k.sVE, k.sSE
    sCK, sCV, sCN, sPL = k.sCK, k.sCV, k.sCN, k.sPL

    idb = k.sb("idb", [128, 128], BF16).ap()
    idf = k.sb("idf", [128, 128], F32).ap()
    onb = k.sb("onb", [128, 1], BF16).ap()
    onfr = k.sb("onfr", [1, 128], F32).ap()
    erp = k.sb("erp", [8, 128], BF16).ap()
    epsS = k.sb("epsS", [128, 1], F32).ap()
    gw1T = k.sb("gw1T", [128, 32 * 32], BF16)
    gw2T = k.sb("gw2T", [32, 8], BF16).ap()
    gvb = k.sb("gvb", [128, 104], F32).ap()
    issb = k.sb("issb", [128, NTB * 16], BF16)
    g12 = k.sb("g12", [128, NTB * 2], F32)
    cntS = k.sb("cntS", [1, 16], F32).ap()
    alwS = k.sb("alwS", [1, 16], F32).ap()
    alwB = k.sb("alwB", [128, 16], F32).ap()
    wTall = k.sb("wTall", [32, NTB * 128], BF16)
    x0m = k.sb("x0m", [128, NTB], F32)
    st = k.sb("stats", [128, 16], F32)
    S1t, S2t, Mt, MSQt, VARt, SQt, RSt, NBt = (st.ap()[:, i:i + 1] for i in range(8))

    ARENA_BYTES = 131072
    SIDE_BYTES = 65536
    arena = k.sb("arena", [128, ARENA_BYTES], U8)
    side = k.sb("side", [128, SIDE_BYTES], U8)
    assert nc.sbuf_bytes_remaining > 2048, nc.sbuf_bytes_remaining

    def AR(off, dtype, cols, parts=128):
        return k.carve(arena, off, dtype, cols, parts)

    def SD(off, dtype, cols, parts=128):
        return k.carve(side, off, dtype, cols, parts)

    xlnT = AR(0, BF16, 32 * 2048)
    ctxT = AR(0, BF16, 32 * 2048)
    xfT = AR(0, BF16, 32 * 2048)

    def xlnT_s(i, c0, n):
        return xlnT[:, i * 2048 + c0: i * 2048 + c0 + n]

    p = [nc.alloc_psum_tensor("p%d" % i, [128, 512], F32) for i in range(5)]
    p5 = nc.alloc_psum_tensor("p5", [128, 1024], BF16)
    p6 = nc.alloc_psum_tensor("p6", [128, 512], F32)
    p7 = nc.alloc_psum_tensor("p7", [128, 512], F32)
    assert nc.psum_banks_remaining == 0 or nc.psum_banks_remaining >= 0

    k.dma("sync", idb, identb.ap(), sQX)
    k.dma("sync", idf, identf.ap(), sQX)
    k.dma("sync", onb, onesb.ap(), sQX)
    k.dma("sync", onfr, onesfr.ap(), sQX)
    k.dma("sync", erp, erep.ap(), sQX)
    k.dma("sync", epsS, epsd.ap(), sQX)
    k.fence()
    if upto == "P0":
        return k

    x0f2b = [SD(0, F32, 4096), SD(38912, F32, 4096)]
    xlnb = SD(16384, BF16, 4096)
    junkA = SD(55296, BF16, 4096)
    wstr = [SD(0, BF16, 32 * 128), SD(8192, BF16, 32 * 128)]
    accT = SD(16384, BF16, 2048)
    ro_t1 = SD(20480, BF16, 2048)
    cxS = SD(24576, BF16, 2048)
    ssS = SD(28672, BF16, 2048)
    vtmp = SD(32768, BF16, 512)
    vtok = SD(33792, BF16, 512)
    qsw = SD(34816, BF16, 2048)

    def wstr_tile(buf, i):
        return wstr[buf][:, i * 128:(i + 1) * 128]

    def ln_build(src_dram, tag):
        save_means = (tag == "q")
        ve_xln_done = [0] * NTB
        tm_tr_done = [0] * NTB
        for tb in range(NTB):
            x0f = x0f2b[tb % 2]
            if tb > 1:
                k.wait("sync", sSE, -ve_xln_done[tb - 2])
            ldv = k.dma("sync", x0f, src_dram.ap()[tb * 128:(tb + 1) * 128, :], sQX)
            k.wait("scalar", sQX, ldv)
            k.em("scalar", "activation", junkA, x0f, AF.Square, inc=[(sSE, 1)],
                 accum_out=S2t)
            se_sq = sSE.inc()
            k.wait("vector", sQX, ldv)
            k.em("vector", "reduce_sum", S1t, x0f, axis=AXX, inc=[(sVE, 1)])
            sVE.inc()
            k.wait("vector", sVE)
            k.em("vector", "tensor_scalar", Mt, S1t, INV_D, None, ALU.mult,
                 inc=[(sVE, 1)])
            sVE.inc()
            k.wait("vector", sVE)
            k.em("vector", "tensor_scalar", MSQt, Mt, Mt, None, ALU.mult,
                 inc=[(sVE, 1)])
            sVE.inc()
            if save_means:
                k.em("vector", "tensor_copy", x0m.ap()[:, tb:tb + 1], Mt,
                     inc=[(sVE, 1)])
                sVE.inc()
            k.wait("vector", sSE, se_sq)
            k.em("vector", "tensor_scalar", VARt, S2t, INV_D, None, ALU.mult)
            k.wait("vector", sVE)
            k.em("vector", "tensor_tensor", VARt, VARt, MSQt, ALU.subtract,
                 inc=[(sVE, 1)])
            sVE.inc()
            k.wait("scalar", sVE)
            k.em("scalar", "activation", SQt, VARt, AF.Sqrt, bias=epsS,
                 inc=[(sSE, 1)])
            sSE.inc()
            k.wait("vector", sSE)
            k.em("vector", "reciprocal", RSt, SQt, inc=[(sVE, 1)])
            sVE.inc()
            k.wait("vector", sVE)
            k.em("vector", "tensor_scalar", NBt, Mt, RSt, -1.0,
                 ALU.mult, ALU.mult, inc=[(sVE, 1)])
            ve_nb = sVE.inc()
            k.wait("scalar", sVE, ve_nb)
            if tb > 0:
                k.wait("scalar", sTM, tm_tr_done[tb - 1])
            k.em("scalar", "activation", xlnb, x0f, AF.Identity, bias=NBt,
                 scale=RSt, inc=[(sSE, 1)])
            ve_xln_done[tb] = -sSE.inc()
            for grp in range(8):
                half = grp % 2
                k.wait("tensor", sVE)
                k.wait("tensor", sSE, -ve_xln_done[tb])
                for q in range(4):
                    i = grp * 4 + q
                    k.em("tensor", "transpose",
                         p5[:, half * 512 + q * 128: half * 512 + (q + 1) * 128],
                         xlnb[:, i * 128:(i + 1) * 128], idb,
                         inc=[(sTM, 1)] if q == 3 else None)
                tm_tr_done[tb] = sTM.inc()
                k.wait("vector", sTM, tm_tr_done[tb])
                dst3 = xlnT.rearrange("p (a t) -> p a t", t=2048)[
                    :, grp * 4:(grp + 1) * 4, tb * 128:(tb + 1) * 128]
                src3 = p5[:, half * 512:(half + 1) * 512].rearrange(
                    "p (a t) -> p a t", a=4)
                k.em("vector", "tensor_copy", dst3, src3, inc=[(sVE, 1)])
                sVE.inc()
        k.fence()

    def proj_phase(w_dram, n_oc, oc_base_off, store_kind):
        pass

    ln_build(xkv, "kv")
    ldcx = k.dma("sync", cxS, cxk.ap(), sQX)
    ldss = k.dma("sync", ssS, ssk.ap(), sQX)

    def do_proj(oc_idx, w_dram, w_col0, is_k, is_v, is_q, head_idx):
        buf = oc_idx % 2
        if do_proj.tm_done[buf] > 0:
            k.wait("gpsimd", sTM, do_proj.tm_done[buf])
        src = w_dram.ap()[:, w_col0:w_col0 + 128].rearrange(
            "(a p) c -> p a c", p=128)
        dst = wstr[buf][:, :].rearrange("p (a c) -> p a c", c=128)
        ldw = k.dma("gpsimd", dst, src, sQW)
        for tc in range(NTCH):
            bank = p[(oc_idx * NTCH + tc) % 2]
            if do_proj.se_bank[(oc_idx * NTCH + tc) % 2] > 0:
                k.wait("tensor", sSE, do_proj.se_bank[(oc_idx * NTCH + tc) % 2])
            k.wait("tensor", sQW, ldw)
            for i in range(32):
                k.em("tensor", "matmul", bank[:], wstr_tile(buf, i),
                     xlnT_s(i, tc * 512, 512),
                     start=(i == 0), stop=(i == 31),
                     inc=[(sTM, 1)] if i == 31 else None)
            mm_done = sTM.inc()
            if tc == NTCH - 1:
                do_proj.tm_done[buf] = mm_done
            k.wait("scalar", sTM, mm_done)
            if is_v:
                k.em("scalar", "activation", vtmp, bank[:], AF.Copy,
                     inc=[(sSE, 1)])
                do_proj.se_bank[(oc_idx * NTCH + tc) % 2] = sSE.inc()
                k.wait("tensor", sSE)
                if do_proj.v_ev > 0:
                    k.wait("tensor", sVE, do_proj.v_ev)
                for q in range(4):
                    k.em("tensor", "transpose", p5[:, q * 128:(q + 1) * 128],
                         vtmp[:, q * 128:(q + 1) * 128], idb,
                         inc=[(sTM, 1)] if q == 3 else None)
                ttr = sTM.inc()
                k.wait("vector", sTM, ttr)
                if do_proj.v_st > 0:
                    k.wait("vector", sQS, do_proj.v_st)
                k.em("vector", "tensor_copy", vtok, p5[:, 0:512],
                     inc=[(sVE, 1)])
                do_proj.v_ev = sVE.inc()
                k.wait("scalar", sVE, do_proj.v_ev)
                dstv = vcc4[head_idx, tc * 4:(tc + 1) * 4].rearrange(
                    "a p d -> p a d")
                srcv = vtok.rearrange("p (a d) -> p a d", a=4)
                do_proj.v_st = k.dma("scalar", dstv, srcv, sQS)
            else:
                if tc == 0 and do_proj.acc_busy > 0:
                    k.wait("scalar", sQS, do_proj.acc_busy)
                k.em("scalar", "activation", accT[:, tc * 512:(tc + 1) * 512],
                     bank[:], AF.Copy, inc=[(sSE, 1)])
                do_proj.se_bank[(oc_idx * NTCH + tc) % 2] = sSE.inc()
        if is_v:
            return
        k.wait("scalar", sSE)
        k.dma("scalar", qsw[0:64, :], accT[64:128, :], sQS)
        swv = k.dma("scalar", qsw[64:128, :], accT[0:64, :], sQS)
        k.wait("vector", sQS, swv)
        k.em("vector", "tensor_tensor", ro_t1, qsw, ssS, ALU.mult,
             inc=[(sVE, 1)])
        sVE.inc()
        k.em("vector", "tensor_tensor", accT, accT, cxS, ALU.mult,
             inc=[(sVE, 1)])
        sVE.inc()
        k.wait("vector", sVE)
        k.em("vector", "tensor_tensor", accT, accT, ro_t1, ALU.add,
             inc=[(sVE, 1)])
        rope_done = sVE.inc()
        k.wait("scalar", sVE, rope_done)
        if is_q:
            for tc in range(NTCH):
                do_proj.acc_busy = k.dma(
                    "scalar", qscr[head_idx, tc],
                    accT[:, tc * 512:(tc + 1) * 512], sQS)
        else:
            for tc in range(NTCH):
                do_proj.acc_busy = k.dma(
                    "scalar", kcc4[head_idx, tc],
                    accT[:, tc * 512:(tc + 1) * 512], sQS)

    do_proj.tm_done = [0, 0]
    do_proj.se_bank = [0, 0]
    do_proj.acc_busy = 0
    do_proj.v_ev = 0
    do_proj.v_st = 0

    k.wait("vector", sQX, ldss)
    for g in range(NKV):
        do_proj(g, wk, g * 128, True, False, False, g)
    for g in range(NKV):
        do_proj(NKV + g, wv, g * 128, False, True, False, g)
    k.wait("gpsimd", sQS)
    k.em("gpsimd", "collective_compute", "AllGather", ALU.bypass,
         [[0, 1], [2, 3], [4, 5], [6, 7]],
         [kcc_in.ap().opt()], [kcc_out.ap().opt()], inc=[(sCK, 1)])
    sCK.inc()
    k.em("gpsimd", "collective_compute", "AllGather", ALU.bypass,
         [[0, 1], [2, 3], [4, 5], [6, 7]],
         [vcc_in.ap().opt()], [vcc_out.ap().opt()], inc=[(sCV, 1)])
    sCV.inc()
    k.fence()
    if upto == "A1":
        return k

    ln_build(xq, "q")
    ldcx = k.dma("sync", cxS, cxq.ap(), sQX)
    ldss = k.dma("sync", ssS, ssq.ap(), sQX)
    k.wait("vector", sQX, ldss)
    do_proj.tm_done = [0, 0]
    do_proj.se_bank = [0, 0]
    do_proj.acc_busy = 0
    for h in range(NH):
        do_proj(h, wq, h * 128, False, False, True, h)
    k.fence()
    if upto == "A2":
        return k

    Kg = [SD(0, BF16, 4096), SD(8192, BF16, 4096)]
    Vg = [SD(16384, BF16, 4096), SD(24576, BF16, 4096)]
    Qt = [SD(32768, BF16, 512), SD(33792, BF16, 512)]
    Pt = [SD(34816, BF16, 512), SD(35840, BF16, 512), SD(36864, BF16, 512),
          SD(53248, BF16, 512), SD(54272, BF16, 512)]
    rbS = SD(37888, F32, 512)
    ctxo = [SD(39936, BF16, 512), SD(40960, BF16, 512)]
    maskS = SD(43008, BF16, 8 * 512)
    recS = SD(51200, F32, 512, parts=1)

    mval = k.dma("sync", maskS.rearrange("p (a c) -> p a c", a=8),
                 masks.ap().rearrange("a p c -> p a c"), sQX)
    k.wait("vector", sQX, mval)
    k.wait("tensor", sCK, 1)
    k.wait("tensor", sCV, 1)
    k.wait("gpsimd", sCK, 1)
    k.wait("gpsimd", sCV, 1)

    kg_ld = [0, 0]
    kg_free = [0, 0]
    def load_g(g):
        buf = g % 2
        if kg_free[buf] > 0:
            k.wait("gpsimd", sTM, kg_free[buf])
        v = 0
        for r in range(2):
            for tcl in range(NTCH):
                v = k.dma("gpsimd", Kg[buf][:, (r * 4 + tcl) * 512:
                                            (r * 4 + tcl + 1) * 512],
                          kcc_o4[r, g, tcl], sQW)
            for q4 in range(4):
                dstv = Vg[buf][:, (r * 16 + q4 * 4) * 128:
                               (r * 16 + q4 * 4 + 4) * 128].rearrange(
                    "p (a d) -> p a d", a=4)
                srcv = vcc_o4[r, g, q4 * 4:(q4 + 1) * 4].rearrange(
                    "a p d -> p a d")
                v = k.dma("gpsimd", dstv, srcv, sQW)
        kg_ld[buf] = v

    load_g(0)
    qt_ld = [0, 0]
    qt_free = [0, 0]
    se_p = [0] * 5
    ve_p = [0] * 5
    tm_p = [0] * 5
    se_sbank = [0, 0]
    ve_ctx = [0, 0]
    qx_ctx = [0, 0]
    kcount = 0
    lnT = SD(57344, F32, 512, parts=1)

    def qt_load(it):
        h_, tc_ = (it // NTCH), it % NTCH
        qb_ = it % 2
        if qt_free[qb_] > 0:
            k.wait("sync", sTM, qt_free[qb_])
        qt_ld[qb_] = k.dma("sync", Qt[qb_], qscr[h_, tc_], sQX)

    ep_state = {
        "ep_prev": None,
        "tm_sum_prev": 0,
        "se_ln_prev": 0,
        "se_rbs_prev": 0,
        "ve_ctxev_prev": 0,
        "tm_bc_prev": 0,
    }

    def ep_head(it, tm_sum):
        k.wait("scalar", sTM, tm_sum)
        if ep_state["tm_bc_prev"] > 0:
            k.wait("scalar", sTM, ep_state["tm_bc_prev"])
        k.em("scalar", "activation", lnT, p7[0:1, :], AF.Ln, inc=[(sSE, 1)])
        ep_state["se_ln_prev"] = sSE.inc()
        k.wait("scalar", sSE)
        k.em("scalar", "activation", recS, lnT, AF.Exp, scale=-1.0,
             inc=[(sSE, 1)])
        return {"it": it, "se_exp": sSE.inc()}

    def ep_bcast(ep):
        k.wait("tensor", sSE, ep["se_exp"])
        if ep_state["se_rbs_prev"] > 0:
            k.wait("tensor", sSE, ep_state["se_rbs_prev"])
        k.em("tensor", "matmul", p6[:], onfr, recS,
             start=True, stop=True, inc=[(sTM, 1)])
        ep["tm_bc"] = sTM.inc()
        ep_state["tm_bc_prev"] = ep["tm_bc"]

    def ep_rbs(ep):
        k.wait("scalar", sTM, ep["tm_bc"])
        if ep_state["ve_ctxev_prev"] > 0:
            k.wait("scalar", sVE, ep_state["ve_ctxev_prev"])
        k.em("scalar", "activation", rbS, p6[:], AF.Copy, inc=[(sSE, 1)])
        ep["se_rbs"] = sSE.inc()
        ep_state["se_rbs_prev"] = ep["se_rbs"]

    def ep_ctxev(ep):
        it = ep["it"]
        k.wait("vector", sSE, ep["se_rbs"])
        if qx_ctx[it % 2] > 0:
            k.wait("vector", sQX, qx_ctx[it % 2])
        k.em("vector", "tensor_tensor", ctxo[it % 2], p[2 + it % 2][:], rbS,
             ALU.mult, inc=[(sVE, 1)])
        ve_ctx[it % 2] = sVE.inc()
        ep_state["ve_ctxev_prev"] = ve_ctx[it % 2]
        k.wait("sync", sVE, ve_ctx[it % 2])
        qx_ctx[it % 2] = k.dma("sync", ctxscr[it // NTCH, it % NTCH],
                               ctxo[it % 2], sQX)

    qt_load(0)
    for g in range(NKV):
        if g + 1 < NKV:
            load_g(g + 1)
        buf = g % 2
        k.wait("tensor", sQW, kg_ld[buf])
        for hh in range(4):
            h = 4 * g + hh
            for tc in range(NTCH):
                it = h * NTCH + tc
                qb = it % 2
                qv = qt_ld[qb]
                if it + 1 < NH * NTCH:
                    qt_load(it + 1)
                nkb = 8 * tc + 8
                cbank = p[2 + (it % 2)]
                k.wait("tensor", sQX, qv)

                def s_mm(kb):
                    sb_ = kb % 2
                    if se_sbank[sb_] > 0:
                        k.wait("tensor", sSE, se_sbank[sb_])
                    k.em("tensor", "matmul", p[sb_][:],
                         Kg[buf][:, kb * 128:(kb + 1) * 128], Qt[qb],
                         start=True, stop=True, inc=[(sTM, 1)])
                    v = sTM.inc()
                    if kb == nkb - 1:
                        qt_free[qb] = v
                    return v

                mm_s = [0] * nkb
                mm_s[0] = s_mm(0)
                for kb in range(nkb):
                    sl = kcount % 5
                    kcount += 1
                    sb_ = (kb % 2)
                    sbank = p[sb_]
                    if kb + 1 < nkb:
                        mm_s[kb + 1] = s_mm(kb + 1)
                    if kb == 4 and ep_state["ep_prev"] is not None:
                        ep_bcast(ep_state["ep_prev"])
                    if kb == 6 and ep_state["ep_prev"] is not None:
                        ep_rbs(ep_state["ep_prev"])
                        ep_ctxev(ep_state["ep_prev"])
                        ep_state["ep_prev"] = None
                    k.wait("scalar", sTM, mm_s[kb])
                    if tm_p[sl] > 0:
                        k.wait("scalar", sTM, tm_p[sl])
                    if ve_p[sl] > 0:
                        k.wait("scalar", sVE, ve_p[sl])
                    k.em("scalar", "activation", Pt[sl], sbank[:], AF.Exp,
                         scale=SCALE, inc=[(sSE, 1)])
                    se_p[sl] = sSE.inc()
                    se_sbank[sb_] = se_p[sl]
                    r = kb - 8 * tc
                    if r >= 0:
                        k.wait("vector", sSE, se_p[sl])
                        k.em("vector", "tensor_tensor", Pt[sl], Pt[sl],
                             maskS[:, r * 512:(r + 1) * 512], ALU.mult,
                             inc=[(sVE, 1)])
                        ve_p[sl] = sVE.inc()
                        k.wait("tensor", sVE, ve_p[sl])
                    else:
                        k.wait("tensor", sSE, se_p[sl])
                    if kb == 0 and ve_ctx[it % 2] > 0:
                        k.wait("tensor", sVE, ve_ctx[it % 2])
                    if kb == 0 and ep_state["se_ln_prev"] > 0:
                        k.wait("tensor", sSE, ep_state["se_ln_prev"])
                    k.em("tensor", "matmul", cbank[:],
                         Vg[buf][:, kb * 128:(kb + 1) * 128], Pt[sl],
                         start=(kb == 0), stop=(kb == nkb - 1),
                         skip_group_check=True)
                    k.em("tensor", "matmul", p7[0:1, :], onb, Pt[sl],
                         start=(kb == 0), stop=(kb == nkb - 1),
                         skip_group_check=True, inc=[(sTM, 1)])
                    tm_p[sl] = sTM.inc()
                if tc == NTCH - 1 and hh == 3:
                    kg_free[buf] = tm_p[(kcount - 1) % 5]
                ep_state["ep_prev"] = ep_head(it, tm_p[(kcount - 1) % 5])
    ep_bcast(ep_state["ep_prev"])
    ep_rbs(ep_state["ep_prev"])
    ep_ctxev(ep_state["ep_prev"])
    k.fence()
    if upto == "B":
        return k

    wos = [SD(0, BF16, 32 * 128), SD(8192, BF16, 32 * 128)]
    scb = [SD(16384, BF16, 512), SD(17408, BF16, 512)]
    gw1s = SD(18432, BF16, 4096, parts=32)
    gw2s = SD(26624, BF16, 32, parts=8)
    gvecS = SD(26752, F32, 104, parts=1)

    for h in range(NH):
        k.dma("sync",
              ctxT[:, h * 2048:(h + 1) * 2048].rearrange(
                  "p (a t) -> p a t", a=4),
              ctxscr[h].rearrange("a p t -> p a t"), sQX)
    ld_ctx = sQX.n

    ldg1 = k.dma("gpsimd", gw1s, gw1.ap(), sQW)
    ldg2 = k.dma("gpsimd", gw2s, gw2.ap(), sQW)
    ldgv = k.dma("gpsimd", gvecS, gvec.ap(), sQW)
    k.wait("tensor", sQW, ldg1)
    for grp in range(4):
        half = grp % 2
        k.wait("tensor", sVE)
        for q in range(8):
            i = grp * 8 + q
            k.em("tensor", "transpose",
                 p5[0:128, half * 512 + q * 32: half * 512 + (q + 1) * 32],
                 gw1s[:, i * 128:(i + 1) * 128], idb[0:32, 0:32],
                 inc=[(sTM, 1)] if q == 7 else None)
        tg = sTM.inc()
        k.wait("vector", sTM, tg)
        dst3 = gw1T.ap().rearrange("p (a c) -> p a c", c=32)[
            :, grp * 8:(grp + 1) * 8, :]
        src3 = p5[:, half * 512: half * 512 + 256].rearrange(
            "p (a c) -> p a c", a=8)
        k.em("vector", "tensor_copy", dst3, src3, inc=[(sVE, 1)])
        sVE.inc()
    k.wait("tensor", sQW, ldg2)
    k.wait("tensor", sVE)
    k.em("tensor", "transpose", p5[0:32, 0:8], gw2s, idb[0:8, 0:8],
         inc=[(sTM, 1)])
    tg2 = sTM.inc()
    k.wait("vector", sTM, tg2)
    k.em("vector", "tensor_copy", gw2T, p5[0:32, 0:8], inc=[(sVE, 1)])
    sVE.inc()
    k.wait("tensor", sQW, ldgv)
    k.em("tensor", "matmul", p6[:, 0:104], onfr, gvecS, start=True, stop=True,
         inc=[(sTM, 1)])
    tgv = sTM.inc()
    k.wait("vector", sTM, tgv)
    k.em("vector", "tensor_copy", gvb, p6[:, 0:104], inc=[(sVE, 1)])
    sVE.inc()

    if upto == "Cp":
        return k
    wo_tm = [0, 0]
    wo_se = [0, 0]
    wo_qs = [0, 0]
    nob_c = 2 if upto == "Cq" else 32
    nob_c = int(os.environ.get("TRNC_OBS", nob_c))
    ser_c = os.environ.get("TRNC_SERIAL", "") == "1"
    for ob in range(nob_c):
        buf = ob % 2
        if wo_tm[buf] > 0:
            k.wait("gpsimd", sTM, wo_tm[buf])
        src = wo.ap()[:, ob * 128:(ob + 1) * 128].rearrange(
            "(a p) c -> p a c", p=128)
        ldw = k.dma("gpsimd", wos[buf][:, :].rearrange("p (a c) -> p a c",
                                                       c=128), src, sQW)
        for tc in range(NTCH):
            it = ob * NTCH + tc
            bank = p[it % 2]
            if ser_c:
                k.fence()
            if wo_se[it % 2] > 0:
                k.wait("tensor", sSE, wo_se[it % 2])
            k.wait("tensor", sQW, ldw)
            k.wait("tensor", sQX, ld_ctx)
            for dv in range(32):
                k.em("tensor", "matmul", bank[:],
                     wos[buf][:, dv * 128:(dv + 1) * 128],
                     ctxT[:, dv * 2048 + tc * 512: dv * 2048 + tc * 512 + 512],
                     start=(dv == 0), stop=(dv == 31),
                     inc=[(sTM, 1)] if dv == 31 else None)
            mm = sTM.inc()
            if tc == NTCH - 1:
                wo_tm[buf] = mm
            k.wait("scalar", sTM, mm)
            if wo_qs[it % 2] > 0:
                k.wait("scalar", sQS, wo_qs[it % 2])
            k.em("scalar", "activation", scb[it % 2], bank[:], AF.Copy,
                 inc=[(sSE, 1)])
            wo_se[it % 2] = sSE.inc()
            k.wait("scalar", sSE, wo_se[it % 2])
            wo_qs[it % 2] = k.dma("scalar", aoscr[ob, tc], scb[it % 2], sQS)
    k.fence()
    if upto in ("C", "Cq"):
        return k

    aost = SD(0, BF16, 4096)
    aotm = SD(8192, BF16, 4096)
    x0f2 = SD(16384, F32, 4096)
    x1f = SD(32768, F32, 4096)
    xftm = SD(49152, BF16, 4096)
    hfb = SD(57344, F32, 32)
    hbb = SD(57600, BF16, 32)
    gfb = SD(57728, F32, 8)
    m1b = SD(57856, F32, 1)
    m2b = SD(57920, F32, 1)
    gml = SD(57984, F32, 8)
    sgd = SD(58048, F32, 1)
    wjf = SD(58112, F32, 8)
    wjf2 = SD(58176, F32, 8)
    wjb = SD(58240, BF16, 8)

    tm_tr_prev = 0
    qw_x1_prev = 0
    pl_add_prev = 0
    for j in range(NTB):
        tcj, sub = j // 4, j % 4
        if j > 0:
            k.wait("sync", sTM, tm_tr_prev)
            k.wait("sync", sPL, pl_add_prev)
        lda = k.dma("sync",
                    aost.rearrange("p (a t) -> p a t", a=32),
                    aoscr[0:32, tcj, :, sub * 128:(sub + 1) * 128]
                    .rearrange("a p t -> p a t"), sQX)
        ldx = k.dma("sync", x0f2, xq.ap()[j * 128:(j + 1) * 128, :], sQX)
        k.wait("tensor", sQX, lda)
        for grp in range(8):
            half = grp % 2
            k.wait("tensor", sVE)
            for q in range(4):
                i = grp * 4 + q
                k.em("tensor", "transpose",
                     p5[:, half * 512 + q * 128: half * 512 + (q + 1) * 128],
                     aost[:, i * 128:(i + 1) * 128], idb,
                     inc=[(sTM, 1)] if q == 3 else None)
            ttr = sTM.inc()
            k.wait("vector", sTM, ttr)
            dst3 = aotm.rearrange("p (a t) -> p a t", a=32)[
                :, grp * 4:(grp + 1) * 4, :]
            src3 = p5[:, half * 512:(half + 1) * 512].rearrange(
                "p (a t) -> p a t", a=4)
            k.em("vector", "tensor_copy", dst3, src3, inc=[(sVE, 1)])
            sVE.inc()
        tm_tr_prev = sTM.n
        k.wait("scalar", sVE)
        k.em("scalar", "activation", xftm, aotm, AF.Square, accum_out=S2t,
             inc=[(sSE, 1)])
        se_sq = sSE.inc()
        k.em("vector", "reduce_sum", S1t, aotm, axis=AXX, inc=[(sVE, 1)])
        sVE.inc()
        k.wait("vector", sVE)
        k.em("vector", "tensor_scalar", Mt, S1t, INV_D, None, ALU.mult,
             inc=[(sVE, 1)])
        sVE.inc()
        k.wait("vector", sVE)
        k.em("vector", "tensor_scalar", MSQt, Mt, Mt, None, ALU.mult,
             inc=[(sVE, 1)])
        sVE.inc()
        k.wait("vector", sSE, se_sq)
        k.em("vector", "tensor_scalar", VARt, S2t, INV_D, None, ALU.mult)
        k.wait("vector", sVE)
        k.em("vector", "tensor_tensor", VARt, VARt, MSQt, ALU.subtract,
             inc=[(sVE, 1)])
        sVE.inc()
        k.wait("scalar", sVE)
        k.em("scalar", "activation", SQt, VARt, AF.Sqrt, bias=epsS,
             inc=[(sSE, 1)])
        sSE.inc()
        k.wait("vector", sSE)
        k.em("vector", "reciprocal", RSt, SQt, inc=[(sVE, 1)])
        sVE.inc()
        k.wait("vector", sVE)
        k.em("vector", "tensor_scalar", NBt, Mt, RSt, -1.0, ALU.mult,
             ALU.mult, inc=[(sVE, 1)])
        ve_nb = sVE.inc()
        k.wait("scalar", sVE, ve_nb)
        if qw_x1_prev > 0:
            k.wait("scalar", sQW, qw_x1_prev)
        if pl_add_prev > 0:
            k.wait("scalar", sPL, pl_add_prev)
        k.em("scalar", "activation", x1f, aotm, AF.Identity, bias=NBt,
             scale=RSt, inc=[(sSE, 1)])
        se_t1 = sSE.inc()
        k.wait("gpsimd", sSE, se_t1)
        k.wait("gpsimd", sQX, ldx)
        k.em("gpsimd", "tensor_tensor", x1f, x1f, x0f2, ALU.add,
             inc=[(sPL, 1)])
        pl_add_prev = sPL.inc()
        qw_x1_prev = k.dma("gpsimd", x1scr[j], x1f, sQW)
        k.wait("scalar", sPL, pl_add_prev)
        k.em("scalar", "activation", aotm, x1f, AF.Square, accum_out=S2t,
             inc=[(sSE, 1)])
        se_sq2 = sSE.inc()
        k.wait("vector", sVE)
        k.em("vector", "tensor_copy", Mt, x0m.ap()[:, j:j + 1],
             inc=[(sVE, 1)])
        sVE.inc()
        k.wait("vector", sVE)
        k.em("vector", "tensor_scalar", MSQt, Mt, Mt, None, ALU.mult,
             inc=[(sVE, 1)])
        sVE.inc()
        k.wait("vector", sSE, se_sq2)
        k.em("vector", "tensor_scalar", VARt, S2t, INV_D, None, ALU.mult)
        k.wait("vector", sVE)
        k.em("vector", "tensor_tensor", VARt, VARt, MSQt, ALU.subtract,
             inc=[(sVE, 1)])
        sVE.inc()
        k.wait("scalar", sVE)
        k.em("scalar", "activation", SQt, VARt, AF.Sqrt, bias=epsS,
             inc=[(sSE, 1)])
        sSE.inc()
        k.wait("vector", sSE)
        k.em("vector", "reciprocal", RSt, SQt, inc=[(sVE, 1)])
        sVE.inc()
        k.wait("vector", sVE)
        k.em("vector", "tensor_scalar", NBt, Mt, RSt, -1.0, ALU.mult,
             ALU.mult, inc=[(sVE, 1)])
        ve_nb2 = sVE.inc()
        k.wait("scalar", sVE, ve_nb2)
        k.em("scalar", "activation", xftm, x1f, AF.Identity, bias=NBt,
             scale=RSt, inc=[(sSE, 1)])
        se_xf = sSE.inc()
        for grp in range(8):
            half = grp % 2
            k.wait("tensor", sVE)
            k.wait("tensor", sSE, se_xf)
            for q in range(4):
                i = grp * 4 + q
                k.em("tensor", "transpose",
                     p5[:, half * 512 + q * 128: half * 512 + (q + 1) * 128],
                     xftm[:, i * 128:(i + 1) * 128], idb,
                     inc=[(sTM, 1)] if q == 3 else None)
            ttr = sTM.inc()
            k.wait("vector", sTM, ttr)
            dst3 = xfT.rearrange("p (a t) -> p a t", t=2048)[
                :, grp * 4:(grp + 1) * 4, j * 128:(j + 1) * 128]
            src3 = p5[:, half * 512:(half + 1) * 512].rearrange(
                "p (a t) -> p a t", a=4)
            k.em("vector", "tensor_copy", dst3, src3, inc=[(sVE, 1)])
            sVE.inc()
        ve_xfT = sVE.n
        k.wait("tensor", sVE, ve_xfT)
        for i in range(32):
            k.em("tensor", "matmul", p6[:, 0:32],
                 xfT.rearrange("p (a t) -> p a t", t=2048)[
                     :, i, j * 128:(j + 1) * 128],
                 gw1T.ap().rearrange("p (a c) -> p a c", c=32)[:, i, :],
                 start=(i == 0), stop=(i == 31),
                 inc=[(sTM, 1)] if i == 31 else None)
        mm_h = sTM.inc()
        k.wait("vector", sTM, mm_h)
        k.em("vector", "tensor_tensor", hfb, p6[:, 0:32], gvb[:, 0:32],
             ALU.add, inc=[(sVE, 1)])
        sVE.inc()
        k.wait("vector", sVE)
        k.em("vector", "reduce_sum", S1t, hfb, axis=AXX, inc=[(sVE, 1)])
        sVE.inc()
        k.wait("vector", sVE)
        k.em("vector", "tensor_scalar", Mt, S1t, 1.0 / 32.0, None, ALU.mult,
             inc=[(sVE, 1)])
        sVE.inc()
        k.wait("vector", sVE)
        k.em("vector", "tensor_scalar", MSQt, Mt, Mt, None, ALU.mult,
             inc=[(sVE, 1)])
        sVE.inc()
        k.wait("scalar", sVE)
        k.em("scalar", "activation", hbb, hfb, AF.Square, accum_out=S2t,
             inc=[(sSE, 1)])
        sSE.inc()
        k.wait("vector", sSE)
        k.em("vector", "tensor_scalar", VARt, S2t, 1.0 / 32.0, None, ALU.mult)
        k.wait("vector", sVE)
        k.em("vector", "tensor_tensor", VARt, VARt, MSQt, ALU.subtract,
             inc=[(sVE, 1)])
        sVE.inc()
        k.wait("scalar", sVE)
        k.em("scalar", "activation", SQt, VARt, AF.Sqrt, bias=epsS,
             inc=[(sSE, 1)])
        sSE.inc()
        k.wait("vector", sSE)
        k.em("vector", "reciprocal", RSt, SQt, inc=[(sVE, 1)])
        sVE.inc()
        k.wait("vector", sVE)
        k.em("vector", "tensor_scalar", hfb, hfb, Mt, RSt, ALU.subtract,
             ALU.mult, inc=[(sVE, 1)])
        sVE.inc()
        k.wait("vector", sVE)
        k.em("vector", "tensor_tensor", hfb, hfb, gvb[:, 32:64], ALU.mult,
             inc=[(sVE, 1)])
        sVE.inc()
        k.wait("vector", sVE)
        k.em("vector", "tensor_tensor", hfb, hfb, gvb[:, 64:96], ALU.add,
             inc=[(sVE, 1)])
        sVE.inc()
        k.wait("vector", sVE)
        k.em("vector", "tensor_scalar", hbb, hfb, 0.0, None, ALU.max,
             inc=[(sVE, 1)])
        ve_h = sVE.inc()
        k.wait("tensor", sVE, ve_h)
        k.em("tensor", "transpose", p5[0:32, 512:640], hbb, idb,
             inc=[(sTM, 1)])
        tm_ht = sTM.inc()
        k.wait("vector", sTM, tm_ht)
        k.em("vector", "tensor_copy", wTall.ap()[0:32, 0:128],
             p5[0:32, 512:640], inc=[(sVE, 1)])
        ve_ht = sVE.inc()
        k.wait("tensor", sVE, ve_ht)
        k.em("tensor", "matmul", p6[:, 64:72], wTall.ap()[0:32, 0:128],
             gw2T, start=True, stop=True, inc=[(sTM, 1)])
        mm_g = sTM.inc()
        k.wait("vector", sTM, mm_g)
        k.em("vector", "tensor_tensor", gfb, p6[:, 64:72], gvb[:, 96:104],
             ALU.add, inc=[(sVE, 1)])
        sVE.inc()
        k.wait("vector", sVE)
        k.em("vector", "reduce_max", m1b, gfb, axis=AXX, inc=[(sVE, 1)])
        sVE.inc()
        k.wait("vector", sVE)
        k.em("vector", "tensor_scalar", issb.ap()[:, j * 16: j * 16 + 8],
             gfb, m1b, None, ALU.is_equal, inc=[(sVE, 1)])
        sVE.inc()
        k.wait("vector", sVE)
        k.em("vector", "tensor_scalar", gml,
             issb.ap()[:, j * 16: j * 16 + 8], -1e30, None, ALU.mult,
             inc=[(sVE, 1)])
        sVE.inc()
        k.wait("vector", sVE)
        k.em("vector", "tensor_tensor", gml, gml, gfb, ALU.add,
             inc=[(sVE, 1)])
        sVE.inc()
        k.wait("vector", sVE)
        k.em("vector", "reduce_max", m2b, gml, axis=AXX, inc=[(sVE, 1)])
        sVE.inc()
        k.wait("vector", sVE)
        k.em("vector", "tensor_scalar", issb.ap()[:, j * 16 + 8: j * 16 + 16],
             gml, m2b, None, ALU.is_equal, inc=[(sVE, 1)])
        sVE.inc()
        k.wait("vector", sVE)
        k.em("vector", "tensor_tensor", sgd, m1b, m2b, ALU.subtract,
             inc=[(sVE, 1)])
        ve_sgd = sVE.inc()
        k.wait("scalar", sVE, ve_sgd)
        k.em("scalar", "activation", g12.ap()[:, 2 * j: 2 * j + 1], sgd,
             AF.Sigmoid, inc=[(sSE, 1)])
        se_g1 = sSE.inc()
        k.wait("vector", sSE, se_g1)
        k.em("vector", "tensor_scalar", g12.ap()[:, 2 * j + 1: 2 * j + 2],
             g12.ap()[:, 2 * j: 2 * j + 1], -1.0, 1.0, ALU.mult, ALU.add,
             inc=[(sVE, 1)])
        sVE.inc()
        k.wait("tensor", sVE)
        k.em("tensor", "matmul", p7[0:1, 0:16], onb,
             issb.ap()[:, j * 16:(j + 1) * 16],
             start=(j == 0), stop=(j == NTB - 1), skip_group_check=True,
             inc=[(sTM, 1)])
        sTM.inc()
    k.wait("vector", sTM)
    k.em("vector", "tensor_copy", cntS, p7[0:1, 0:16], inc=[(sVE, 1)])
    ve_cnt = sVE.inc()
    k.wait("gpsimd", sVE, ve_cnt)
    k.dma("gpsimd", cnt_in.ap(), cntS, sQW)
    k.wait("gpsimd", sQW)
    k.em("gpsimd", "collective_compute", "AllReduce", ALU.add,
         [[0, 1, 2, 3, 4, 5, 6, 7]],
         [cnt_in.ap().opt()], [cnt_out.ap().opt()], inc=[(sCN, 1)])
    sCN.inc()
    k.fence()
    if upto == "C2":
        return k

    asb = SD(0, BF16, 4096)
    bsb = SD(8192, BF16, 4096)
    tmpT = SD(16384, F32, 2048)
    scT = SD(24576, BF16, 2048)
    weT = SD(28672, BF16, 2048)
    wband = [SD(32768, BF16, 4096), SD(40960, BF16, 4096)]
    WTb = [SD(16384, BF16, 4096), SD(0, BF16, 4096)]
    fob = [SD(49152, BF16, 512), SD(50176, BF16, 512)]

    lda_ = k.dma("gpsimd", asb.rearrange("p (a c) -> p a c", c=128),
                 astack.ap().rearrange("(a p) c -> p a c", p=128), sQW)
    ldb_ = k.dma("gpsimd", bsb, bstack.ap(), sQW)
    for tc in range(NTCH):
        bank = p[tc % 2]
        k.wait("tensor", sQW, lda_)
        k.wait("tensor", sVE)
        for i in range(32):
            k.em("tensor", "matmul", bank[:],
                 asb.rearrange("p (a c) -> p a c", c=128)[:, i, :],
                 xfT.rearrange("p (a t) -> p a t", t=2048)[
                     :, i, tc * 512:(tc + 1) * 512],
                 start=(i == 0), stop=(i == 31),
                 inc=[(sTM, 1)] if i == 31 else None)
        mm = sTM.inc()
        k.wait("vector", sTM, mm)
        k.em("vector", "tensor_copy", tmpT[:, tc * 512:(tc + 1) * 512],
             bank[:], inc=[(sVE, 1)])
        sVE.inc()
    k.wait("sync", sCN, 1)
    ldc = k.dma("sync", cntS, cnt_out.ap(), sQX)
    k.wait("vector", sQX, ldc)
    k.em("vector", "tensor_scalar", alwS, cntS, CAPACITY, None, ALU.is_le,
         inc=[(sVE, 1)])
    ve_alw = sVE.inc()
    k.wait("tensor", sVE, ve_alw)
    k.em("tensor", "matmul", p6[:, 0:16], onfr, alwS, start=True, stop=True,
         inc=[(sTM, 1)])
    tm_alw = sTM.inc()
    k.wait("vector", sTM, tm_alw)
    k.em("vector", "tensor_copy", alwB, p6[:, 0:16], inc=[(sVE, 1)])
    sVE.inc()
    for j in range(NTB):
        k.wait("vector", sVE)
        k.em("vector", "tensor_scalar", wjf,
             issb.ap()[:, j * 16: j * 16 + 8],
             g12.ap()[:, 2 * j: 2 * j + 1], None, ALU.mult, inc=[(sVE, 1)])
        sVE.inc()
        k.wait("vector", sVE)
        k.em("vector", "tensor_tensor", wjf, wjf, alwB[:, 0:8], ALU.mult,
             inc=[(sVE, 1)])
        sVE.inc()
        k.em("vector", "tensor_scalar", wjf2,
             issb.ap()[:, j * 16 + 8: j * 16 + 16],
             g12.ap()[:, 2 * j + 1: 2 * j + 2], None, ALU.mult,
             inc=[(sVE, 1)])
        sVE.inc()
        k.wait("vector", sVE)
        k.em("vector", "tensor_tensor", wjf2, wjf2, alwB[:, 8:16], ALU.mult,
             inc=[(sVE, 1)])
        sVE.inc()
        k.wait("vector", sVE)
        k.em("vector", "tensor_tensor", wjb, wjf, wjf2, ALU.add,
             inc=[(sVE, 1)])
        ve_wj = sVE.inc()
        k.wait("tensor", sVE, ve_wj)
        k.em("tensor", "transpose", p5[0:8, 512:640], wjb, idb,
             inc=[(sTM, 1)])
        tm_wt = sTM.inc()
        k.wait("vector", sTM, tm_wt)
        k.em("vector", "tensor_copy",
             wTall.ap()[0:8, j * 128:(j + 1) * 128], p5[0:8, 512:640],
             inc=[(sVE, 1)])
        sVE.inc()
    ve_wtall = sVE.n
    for tc in range(NTCH):
        k.wait("tensor", sVE)
        k.em("tensor", "matmul", p6[:], erp,
             wTall.ap()[0:8, tc * 512:(tc + 1) * 512], start=True, stop=True,
             inc=[(sTM, 1)])
        tm_we = sTM.inc()
        k.wait("vector", sTM, tm_we)
        k.em("vector", "tensor_copy", weT[:, tc * 512:(tc + 1) * 512],
             p6[:], inc=[(sVE, 1)])
        sVE.inc()
    k.wait("vector", sVE)
    k.em("vector", "tensor_tensor", scT, tmpT, weT, ALU.mult, inc=[(sVE, 1)])
    ve_sc = sVE.inc()

    d3_tm = [0, 0]
    d3_wt = [0, 0]
    d3_se = [0, 0]
    d3_qs = [0, 0]
    for ob in range(32):
        buf = ob % 2
        if d3_tm[buf] > 0:
            k.wait("gpsimd", sTM, d3_tm[buf])
        ldw = k.dma("gpsimd", wband[buf],
                    wffn.ap()[ob * 128:(ob + 1) * 128, :], sQW)
        k.wait("tensor", sQW, ldw)
        if ob >= 2:
            k.wait("tensor", sVE, d3_wt[buf])
        for grp in range(8):
            half = grp % 2
            k.wait("tensor", sVE)
            for q in range(4):
                i = grp * 4 + q
                k.em("tensor", "transpose",
                     p5[:, half * 512 + q * 128: half * 512 + (q + 1) * 128],
                     wband[buf][:, i * 128:(i + 1) * 128], idb,
                     inc=[(sTM, 1)] if q == 3 else None)
            ttr = sTM.inc()
            k.wait("vector", sTM, ttr)
            dst3 = WTb[buf].rearrange("p (a c) -> p a c", c=128)[
                :, grp * 4:(grp + 1) * 4, :]
            src3 = p5[:, half * 512:(half + 1) * 512].rearrange(
                "p (a c) -> p a c", a=4)
            k.em("vector", "tensor_copy", dst3, src3, inc=[(sVE, 1)])
            sVE.inc()
        d3_wt[buf] = sVE.n
        k.wait("tensor", sVE, d3_wt[buf])
        if ob == 0:
            k.wait("tensor", sVE, ve_sc)
            k.wait("tensor", sQW, ldb_)
        for tc in range(NTCH):
            it = ob * NTCH + tc
            bank = p[2 + it % 2]
            if d3_se[it % 2] > 0:
                k.wait("tensor", sSE, d3_se[it % 2])
            for i in range(32):
                k.em("tensor", "matmul", bank[:],
                     WTb[buf].rearrange("p (a c) -> p a c", c=128)[:, i, :],
                     xfT.rearrange("p (a t) -> p a t", t=2048)[
                         :, i, tc * 512:(tc + 1) * 512],
                     start=(i == 0), stop=False)
            k.em("tensor", "matmul", bank[:],
                 bsb[:, ob * 128:(ob + 1) * 128],
                 scT[:, tc * 512:(tc + 1) * 512],
                 start=False, stop=True, skip_group_check=True,
                 inc=[(sTM, 1)])
            mm = sTM.inc()
            if tc == NTCH - 1:
                d3_tm[buf] = mm
            k.wait("scalar", sTM, mm)
            if d3_qs[it % 2] > 0:
                k.wait("scalar", sQS, d3_qs[it % 2])
            k.em("scalar", "activation", fob[it % 2], bank[:], AF.Copy,
                 inc=[(sSE, 1)])
            d3_se[it % 2] = sSE.inc()
            k.wait("scalar", sSE, d3_se[it % 2])
            d3_qs[it % 2] = k.dma("scalar", foscr[ob, tc], fob[it % 2], sQS)
    k.fence()
    if upto == "D":
        return k

    fost = SD(0, BF16, 4096)
    fotm = SD(8192, BF16, 4096)
    x1bf = [SD(16384, F32, 4096), SD(32768, F32, 4096)]
    outb = SD(49152, F32, 4096)

    qx_out = 0
    pl_out = [0, 0]
    se_sq_prev = 0
    tm_tr_prevE = 0
    for j in range(NTB):
        tcj, sub = j // 4, j % 4
        if j > 0:
            k.wait("sync", sTM, tm_tr_prevE)
            k.wait("sync", sSE, se_sq_prev)
        ldf = k.dma("sync",
                    fost.rearrange("p (a t) -> p a t", a=32),
                    foscr[0:32, tcj, :, sub * 128:(sub + 1) * 128]
                    .rearrange("a p t -> p a t"), sQX)
        if pl_out[j % 2] > 0:
            k.wait("gpsimd", sPL, pl_out[j % 2])
        ldx1 = k.dma("gpsimd", x1bf[j % 2], x1scr[j], sQW)
        k.wait("tensor", sQX, ldf)
        for grp in range(8):
            half = grp % 2
            k.wait("tensor", sVE)
            for q in range(4):
                i = grp * 4 + q
                k.em("tensor", "transpose",
                     p5[:, half * 512 + q * 128: half * 512 + (q + 1) * 128],
                     fost[:, i * 128:(i + 1) * 128], idb,
                     inc=[(sTM, 1)] if q == 3 else None)
            ttr = sTM.inc()
            k.wait("vector", sTM, ttr)
            dst3 = fotm.rearrange("p (a t) -> p a t", a=32)[
                :, grp * 4:(grp + 1) * 4, :]
            src3 = p5[:, half * 512:(half + 1) * 512].rearrange(
                "p (a t) -> p a t", a=4)
            k.em("vector", "tensor_copy", dst3, src3, inc=[(sVE, 1)])
            sVE.inc()
        tm_tr_prevE = sTM.n
        k.wait("scalar", sVE)
        k.em("scalar", "activation", fost, fotm, AF.Square, accum_out=S2t,
             inc=[(sSE, 1)])
        se_sq = sSE.inc()
        se_sq_prev = se_sq
        k.em("vector", "reduce_sum", S1t, fotm, axis=AXX, inc=[(sVE, 1)])
        sVE.inc()
        k.wait("vector", sVE)
        k.em("vector", "tensor_scalar", Mt, S1t, INV_D, None, ALU.mult,
             inc=[(sVE, 1)])
        sVE.inc()
        k.wait("vector", sVE)
        k.em("vector", "tensor_scalar", MSQt, Mt, Mt, None, ALU.mult,
             inc=[(sVE, 1)])
        sVE.inc()
        k.wait("vector", sSE, se_sq)
        k.em("vector", "tensor_scalar", VARt, S2t, INV_D, None, ALU.mult)
        k.wait("vector", sVE)
        k.em("vector", "tensor_tensor", VARt, VARt, MSQt, ALU.subtract,
             inc=[(sVE, 1)])
        sVE.inc()
        k.wait("scalar", sVE)
        k.em("scalar", "activation", SQt, VARt, AF.Sqrt, bias=epsS,
             inc=[(sSE, 1)])
        sSE.inc()
        k.wait("vector", sSE)
        k.em("vector", "reciprocal", RSt, SQt, inc=[(sVE, 1)])
        sVE.inc()
        k.wait("vector", sVE)
        k.em("vector", "tensor_scalar", NBt, Mt, RSt, -1.0, ALU.mult,
             ALU.mult, inc=[(sVE, 1)])
        ve_nb = sVE.inc()
        k.wait("scalar", sVE, ve_nb)
        if qx_out > 0:
            k.wait("scalar", sQX, qx_out)
        k.em("scalar", "activation", outb, fotm, AF.Identity, bias=NBt,
             scale=RSt, inc=[(sSE, 1)])
        se_t1 = sSE.inc()
        k.wait("gpsimd", sSE, se_t1)
        k.wait("gpsimd", sQW, ldx1)
        k.em("gpsimd", "tensor_tensor", outb, outb, x1bf[j % 2], ALU.add,
             inc=[(sPL, 1)])
        pl_out[j % 2] = sPL.inc()
        k.wait("sync", sPL, pl_out[j % 2])
        qx_out = k.dma("sync", out_d.ap()[j * 128:(j + 1) * 128, :],
                       outb, sQX)
    k.fence()

    return k


def _host_constants(s):
    bf = ml_dtypes.bfloat16
    qblocks = np.array([2 * i + s for i in range(NTB)])
    qpos = (qblocks[:, None] * 128 + np.arange(128)[None, :]).reshape(-1)
    kpos = np.arange(TPC) + 2048 * s
    inv = 1.0 / (500000.0 ** (np.arange(0, 128, 2, dtype=np.float64) / 128.0))

    def tabs(pos):
        fr = pos[:, None].astype(np.float64) * inv[None, :]
        c = np.cos(fr).T.astype(np.float32)
        sn = np.sin(fr).T.astype(np.float32)
        cx = np.vstack([c, c]).astype(bf)
        ss = np.vstack([-sn, sn]).astype(bf)
        return cx, ss

    cxq, ssq = tabs(qpos)
    cxk, ssk = tabs(kpos)
    kap = np.arange(128)[:, None]
    masks = np.zeros((8, 128, 512), dtype=np.float32)
    for r in range(8):
        for j in range(4):
            tau = np.arange(128)[None, :]
            valid = (128 * r + kap) <= (128 * (2 * j + s) + tau)
            masks[r, :, j * 128:(j + 1) * 128] = valid
    return {
        "cxq": np.ascontiguousarray(cxq), "ssq": np.ascontiguousarray(ssq),
        "cxk": np.ascontiguousarray(cxk), "ssk": np.ascontiguousarray(ssk),
        "masks": masks.astype(bf),
        "identb": np.eye(128, dtype=bf),
        "identf": np.eye(128, dtype=np.float32),
        "onesb": np.ones((128, 1), dtype=bf),
        "onesfr": np.ones((1, 128), dtype=np.float32),
        "epsd": np.full((128, 1), 1e-5, dtype=np.float32),
        "erep": (np.repeat(np.eye(8, dtype=np.float32), 16, axis=1)
                 / 16.0).astype(bf),
    }, qblocks


def kernel(L0_attn_x, L0_attn_Wq, L0_attn_Wk, L0_attn_Wv, L0_attn_Wo,
           L0_ffn_weight, L0_ffn_lora_A, L0_ffn_lora_B, L0_ffn_gw1,
           L0_ffn_gb1, L0_ffn_ln_gamma, L0_ffn_ln_beta, L0_ffn_gw2,
           L0_ffn_gb2):
    x = np.ascontiguousarray(np.asarray(L0_attn_x, dtype=np.float32))
    wq_ = np.ascontiguousarray(np.asarray(L0_attn_Wq, dtype=np.float32))
    wk_ = np.ascontiguousarray(np.asarray(L0_attn_Wk, dtype=np.float32))
    wv_ = np.ascontiguousarray(np.asarray(L0_attn_Wv, dtype=np.float32))
    wo_ = np.ascontiguousarray(np.asarray(L0_attn_Wo, dtype=np.float32))
    wf_ = np.ascontiguousarray(np.asarray(L0_ffn_weight, dtype=np.float32))
    astack = np.ascontiguousarray(
        np.asarray(L0_ffn_lora_A, dtype=np.float32).transpose(1, 0, 2)
        .reshape(D, 128))
    bstack = np.ascontiguousarray(
        np.asarray(L0_ffn_lora_B, dtype=np.float32).reshape(128, D))
    gw1_ = np.ascontiguousarray(np.asarray(L0_ffn_gw1, dtype=np.float32))
    gw2_ = np.ascontiguousarray(np.asarray(L0_ffn_gw2, dtype=np.float32))
    gvec = np.concatenate([
        np.asarray(L0_ffn_gb1, dtype=np.float32).ravel(),
        np.asarray(L0_ffn_ln_gamma, dtype=np.float32).ravel(),
        np.asarray(L0_ffn_ln_beta, dtype=np.float32).ravel(),
        np.asarray(L0_ffn_gb2, dtype=np.float32).ravel()])[None, :]
    gvec = np.ascontiguousarray(gvec)

    kb = build_program()
    nc = kb.finalize()

    in_maps = []
    qb_by_s = {}
    for c in range(NCORES):
        b, s = c // 2, c % 2
        consts, qblocks = _host_constants(s)
        qb_by_s[s] = qblocks
        xb = x[b].reshape(32, 128, D)
        m = {
            "xq": np.ascontiguousarray(xb[qblocks].reshape(TPC, D)),
            "xkv": np.ascontiguousarray(x[b, 2048 * s: 2048 * s + TPC]),
            "wq": wq_, "wk": wk_, "wv": wv_, "wo": wo_, "wffn": wf_,
            "astack": astack, "bstack": bstack,
            "gw1": gw1_, "gw2": gw2_, "gvec": gvec,
        }
        m.update(consts)
        in_maps.append(m)

    res = run_bass_kernel_spmd(nc, in_maps, core_ids=list(range(NCORES)))

    out = np.empty((B, T, D), dtype=np.float32)
    for c in range(NCORES):
        b, s = c // 2, c % 2
        shard = res.results[c]["out"].reshape(NTB, 128, D)
        ob = out[b].reshape(32, 128, D)
        ob[qb_by_s[s]] = shard
    return out


# version 10
# speedup vs baseline: 1.0091x; 1.0068x over previous
import os
import numpy as np
import ml_dtypes
import concourse.bass as bass
import concourse.mybir as mybir
from concourse.bass_utils import run_bass_kernel_spmd

F32 = mybir.dt.float32
BF16 = mybir.dt.bfloat16
U8 = mybir.dt.uint8
AF = mybir.ActivationFunctionType
ALU = mybir.AluOpType
AXX = mybir.AxisListType.X

B, T, D = 4, 4096, 4096
NCORES = 8
TPC = 2048
NH, NKV, HD = 32, 8, 128
NTB, NTCH = 16, 4
SCALE = 1.0 / float(np.sqrt(128.0))
EPS = 1e-5
CAPACITY = 6144.0
INV_D = 1.0 / 4096.0
ENGINES = ("sync", "tensor", "vector", "scalar", "gpsimd")


class S:
    __slots__ = ("h", "n")

    def __init__(self, h):
        self.h = h
        self.n = 0

    def inc(self, k=1):
        self.n += k
        return self.n


class KB:
    def __init__(self, dbg=()):
        self.nc = bass.Bass("TRN2", target_bir_lowering=False, debug=False)
        self.ops = {e: [] for e in ENGINES}
        self.dbg = set(dbg)

    def em(self, e, meth, *args, inc=None, **kw):
        self.ops[e].append((meth, args, kw, inc or []))

    def wait(self, e, s, v=None):
        self.em(e, "wait_ge", s.h, s.n if v is None else v)

    def dma(self, e, out, in_, s):
        self.em(e, "dma_start", inc=[(s, 16)], out=out, in_=in_)
        return s.inc(16)

    def fence(self):
        self.wait("sync", self.sQX)
        self.wait("gpsimd", self.sQW)
        self.wait("scalar", self.sQS)
        tgt = self.sFE.n + len(ENGINES)
        for e in ENGINES:
            self.em(e, "nop", inc=[(self.sFE, 1)])
        self.sFE.n = tgt
        for e in ENGINES:
            self.wait(e, self.sFE, tgt)

    def finalize(self):
        ops = self.ops

        with self.nc.Block() as block:
            def run_for(name):
                def body(eng):
                    for meth, args, kw, incs in ops[name]:
                        ins = getattr(eng, meth)(*args, **kw)
                        for (sm, v) in incs:
                            ins = ins.then_inc(sm.h, v)
                return body

            block.sync(run_for("sync"))
            block.tensor(run_for("tensor"))
            block.vector(run_for("vector"))
            block.scalar(run_for("scalar"))
            block.gpsimd(run_for("gpsimd"))
        return self.nc

    def sb(self, name, shape, dtype):
        return self.nc.alloc_sbuf_tensor(name, shape, dtype)

    def carve(self, region, off_bytes, dtype, cols, parts=128):
        esz = 2 if dtype == BF16 else 4
        assert off_bytes % esz == 0
        h = region.bitcast(dtype)
        ap = h.ap()[0:parts, off_bytes // esz: off_bytes // esz + cols]
        return ap

    def tap(self, name, ap, shape, dtype):
        if name not in self.dbg:
            return
        t = self.nc.dram_tensor("dbg_" + name, shape, dtype, kind="ExternalOutput").ap()
        self.wait("sync", self.sVE)
        self.wait("sync", self.sSE)
        self.dma("sync", t, ap, self.sQX)


def build_program(dbg=(), upto="E"):
    k = KB(dbg)
    nc = k.nc

    def din(name, shape, dtype=F32):
        return nc.dram_tensor(name, shape, dtype, kind="ExternalInput")

    xq = din("xq", [TPC, D])
    xkv = din("xkv", [TPC, D])
    wq = din("wq", [D, D])
    wk = din("wk", [D, 1024])
    wv = din("wv", [D, 1024])
    wo = din("wo", [D, D])
    wffn = din("wffn", [D, D])
    astack = din("astack", [D, 128])
    bstack = din("bstack", [128, D])
    gw1 = din("gw1", [32, D])
    gw2 = din("gw2", [8, 32])
    gvec = din("gvec", [1, 104])
    cxq = din("cxq", [128, TPC], BF16)
    ssq = din("ssq", [128, TPC], BF16)
    cxk = din("cxk", [128, TPC], BF16)
    ssk = din("ssk", [128, TPC], BF16)
    masks = din("masks", [8, 128, 512], BF16)
    identb = din("identb", [128, 128], BF16)
    onesb = din("onesb", [128, 1], BF16)
    onesfr = din("onesfr", [1, 128], F32)
    erep = din("erep", [8, 128], BF16)
    epsd = din("epsd", [128, 1], F32)
    out_d = nc.dram_tensor("out", [TPC, D], F32, kind="ExternalOutput")

    def dint(name, shape, dtype=BF16):
        return nc.dram_tensor(name, shape, dtype, kind="Internal")

    qscr = dint("qscr", [NH, NTCH, 128, 512])
    kcc_in = dint("kcc_in", [NKV * NTCH * 128, 512])
    kcc_out = dint("kcc_out", [2 * NKV * NTCH * 128, 512])
    vcc_in = dint("vcc_in", [NKV * 16 * 128, 128])
    vcc_out = dint("vcc_out", [2 * NKV * 16 * 128, 128])
    ctxscr = dint("ctxscr", [NH, NTCH, 128, 512])
    aoscr = dint("aoscr", [32, NTCH, 128, 512])
    x1scr = dint("x1scr", [NTB, 128, D])
    foscr = dint("foscr", [32, NTCH, 128, 512])
    cnt_in = dint("cnt_in", [1, 16], F32)
    cnt_out = dint("cnt_out", [1, 16], F32)

    kcc4 = kcc_in.reshape([NKV, NTCH, 128, 512])
    kcc_o4 = kcc_out.reshape([2, NKV, NTCH, 128, 512])
    vcc4 = vcc_in.reshape([NKV, 16, 128, 128])
    vcc_o4 = vcc_out.reshape([2, NKV, 16, 128, 128])

    for s_ in ("sQX", "sQW", "sQS", "sTM", "sVE", "sSE", "sFE", "sCK", "sCV", "sCN", "sPL"):
        setattr(k, s_, S(nc.alloc_semaphore(s_)))
    sQX, sQW, sQS = k.sQX, k.sQW, k.sQS
    sTM, sVE, sSE = k.sTM, k.sVE, k.sSE
    sCK, sCV, sCN, sPL = k.sCK, k.sCV, k.sCN, k.sPL

    idb = k.sb("idb", [128, 128], BF16).ap()
    onb = k.sb("onb", [128, 1], BF16).ap()
    onfr = k.sb("onfr", [1, 128], F32).ap()
    erp = k.sb("erp", [8, 128], BF16).ap()
    epsS = k.sb("epsS", [128, 1], F32).ap()
    gw1T = k.sb("gw1T", [128, 32 * 32], BF16)
    gw2T = k.sb("gw2T", [32, 8], BF16).ap()
    gvb = k.sb("gvb", [128, 104], F32).ap()
    issb = k.sb("issb", [128, NTB * 16], BF16)
    g12 = k.sb("g12", [128, NTB * 2], F32)
    cntS = k.sb("cntS", [1, 16], F32).ap()
    alwS = k.sb("alwS", [1, 16], F32).ap()
    alwB = k.sb("alwB", [128, 16], F32).ap()
    wTall = k.sb("wTall", [32, NTB * 128], BF16)
    x0m = k.sb("x0m", [128, NTB], F32)
    st = k.sb("stats", [128, 16], F32)
    S1t, S2t, Mt, MSQt, VARt, SQt, RSt, NBt = (st.ap()[:, i:i + 1] for i in range(8))

    ARENA_BYTES = 131072
    SIDE_BYTES = 65536
    arena = k.sb("arena", [128, ARENA_BYTES], U8)
    side = k.sb("side", [128, SIDE_BYTES], U8)
    assert nc.sbuf_bytes_remaining > 2048, nc.sbuf_bytes_remaining

    def AR(off, dtype, cols, parts=128):
        return k.carve(arena, off, dtype, cols, parts)

    def SD(off, dtype, cols, parts=128):
        return k.carve(side, off, dtype, cols, parts)

    xlnT = AR(0, BF16, 32 * 2048)
    ctxT = AR(0, BF16, 32 * 2048)
    xfT = AR(0, BF16, 32 * 2048)

    def xlnT_s(i, c0, n):
        return xlnT[:, i * 2048 + c0: i * 2048 + c0 + n]

    p = [nc.alloc_psum_tensor("p%d" % i, [128, 512], F32) for i in range(5)]
    p5 = nc.alloc_psum_tensor("p5", [128, 1024], BF16)
    p6 = nc.alloc_psum_tensor("p6", [128, 512], F32)
    p7 = nc.alloc_psum_tensor("p7", [128, 512], F32)
    assert nc.psum_banks_remaining == 0 or nc.psum_banks_remaining >= 0

    k.dma("sync", idb, identb.ap(), sQX)
    k.dma("sync", onb, onesb.ap(), sQX)
    k.dma("sync", onfr, onesfr.ap(), sQX)
    k.dma("sync", erp, erep.ap(), sQX)
    k.dma("sync", epsS, epsd.ap(), sQX)
    k.fence()
    if upto == "P0":
        return k

    x0f2b = [SD(0, F32, 4096), SD(38912, F32, 4096)]
    xlnb = SD(16384, BF16, 4096)
    junkA = SD(55296, BF16, 4096)
    wstr = [SD(0, BF16, 32 * 128), SD(8192, BF16, 32 * 128)]
    accT = SD(16384, BF16, 2048)
    ro_t1 = SD(20480, BF16, 2048)
    cxS = SD(24576, BF16, 2048)
    ssS = SD(28672, BF16, 2048)
    vtmp = SD(32768, BF16, 512)
    vtok = SD(33792, BF16, 512)
    qsw = SD(34816, BF16, 2048)

    def wstr_tile(buf, i):
        return wstr[buf][:, i * 128:(i + 1) * 128]

    def ln_build(src_dram, tag):
        save_means = (tag == "q")
        ve_xln_done = [0] * NTB
        tm_tr_done = [0] * NTB
        for tb in range(NTB):
            x0f = x0f2b[tb % 2]
            if tb > 1:
                k.wait("sync", sSE, -ve_xln_done[tb - 2])
            ldv = k.dma("sync", x0f, src_dram.ap()[tb * 128:(tb + 1) * 128, :], sQX)
            k.wait("scalar", sQX, ldv)
            k.em("scalar", "activation", junkA, x0f, AF.Square, inc=[(sSE, 1)],
                 accum_out=S2t)
            se_sq = sSE.inc()
            k.wait("vector", sQX, ldv)
            k.em("vector", "reduce_sum", S1t, x0f, axis=AXX, inc=[(sVE, 1)])
            sVE.inc()
            k.wait("vector", sVE)
            k.em("vector", "tensor_scalar", Mt, S1t, INV_D, None, ALU.mult,
                 inc=[(sVE, 1)])
            sVE.inc()
            k.wait("vector", sVE)
            k.em("vector", "tensor_scalar", MSQt, Mt, Mt, None, ALU.mult,
                 inc=[(sVE, 1)])
            sVE.inc()
            if save_means:
                k.em("vector", "tensor_copy", x0m.ap()[:, tb:tb + 1], Mt,
                     inc=[(sVE, 1)])
                sVE.inc()
            k.wait("vector", sSE, se_sq)
            k.em("vector", "tensor_scalar", VARt, S2t, INV_D, None, ALU.mult)
            k.wait("vector", sVE)
            k.em("vector", "tensor_tensor", VARt, VARt, MSQt, ALU.subtract,
                 inc=[(sVE, 1)])
            sVE.inc()
            k.wait("scalar", sVE)
            k.em("scalar", "activation", SQt, VARt, AF.Sqrt, bias=epsS,
                 inc=[(sSE, 1)])
            sSE.inc()
            k.wait("vector", sSE)
            k.em("vector", "reciprocal", RSt, SQt, inc=[(sVE, 1)])
            sVE.inc()
            k.wait("vector", sVE)
            k.em("vector", "tensor_scalar", NBt, Mt, RSt, -1.0,
                 ALU.mult, ALU.mult, inc=[(sVE, 1)])
            ve_nb = sVE.inc()
            k.wait("scalar", sVE, ve_nb)
            if tb > 0:
                k.wait("scalar", sTM, tm_tr_done[tb - 1])
            k.em("scalar", "activation", xlnb, x0f, AF.Identity, bias=NBt,
                 scale=RSt, inc=[(sSE, 1)])
            ve_xln_done[tb] = -sSE.inc()
            for grp in range(8):
                half = grp % 2
                k.wait("tensor", sVE)
                k.wait("tensor", sSE, -ve_xln_done[tb])
                for q in range(4):
                    i = grp * 4 + q
                    k.em("tensor", "transpose",
                         p5[:, half * 512 + q * 128: half * 512 + (q + 1) * 128],
                         xlnb[:, i * 128:(i + 1) * 128], idb,
                         inc=[(sTM, 1)] if q == 3 else None)
                tm_tr_done[tb] = sTM.inc()
                k.wait("vector", sTM, tm_tr_done[tb])
                dst3 = xlnT.rearrange("p (a t) -> p a t", t=2048)[
                    :, grp * 4:(grp + 1) * 4, tb * 128:(tb + 1) * 128]
                src3 = p5[:, half * 512:(half + 1) * 512].rearrange(
                    "p (a t) -> p a t", a=4)
                k.em("vector", "tensor_copy", dst3, src3, inc=[(sVE, 1)])
                sVE.inc()
        k.fence()

    def proj_phase(w_dram, n_oc, oc_base_off, store_kind):
        pass

    ln_build(xkv, "kv")
    ldcx = k.dma("sync", cxS, cxk.ap(), sQX)
    ldss = k.dma("sync", ssS, ssk.ap(), sQX)

    def do_proj(oc_idx, w_dram, w_col0, is_k, is_v, is_q, head_idx):
        buf = oc_idx % 2
        if do_proj.tm_done[buf] > 0:
            k.wait("gpsimd", sTM, do_proj.tm_done[buf])
        src = w_dram.ap()[:, w_col0:w_col0 + 128].rearrange(
            "(a p) c -> p a c", p=128)
        dst = wstr[buf][:, :].rearrange("p (a c) -> p a c", c=128)
        ldw = k.dma("gpsimd", dst, src, sQW)
        for tc in range(NTCH):
            bank = p[(oc_idx * NTCH + tc) % 2]
            if do_proj.se_bank[(oc_idx * NTCH + tc) % 2] > 0:
                k.wait("tensor", sSE, do_proj.se_bank[(oc_idx * NTCH + tc) % 2])
            k.wait("tensor", sQW, ldw)
            for i in range(32):
                k.em("tensor", "matmul", bank[:], wstr_tile(buf, i),
                     xlnT_s(i, tc * 512, 512),
                     start=(i == 0), stop=(i == 31),
                     inc=[(sTM, 1)] if i == 31 else None)
            mm_done = sTM.inc()
            if tc == NTCH - 1:
                do_proj.tm_done[buf] = mm_done
            k.wait("scalar", sTM, mm_done)
            if is_v:
                k.em("scalar", "activation", vtmp, bank[:], AF.Copy,
                     inc=[(sSE, 1)])
                do_proj.se_bank[(oc_idx * NTCH + tc) % 2] = sSE.inc()
                k.wait("tensor", sSE)
                if do_proj.v_ev > 0:
                    k.wait("tensor", sVE, do_proj.v_ev)
                for q in range(4):
                    k.em("tensor", "transpose", p5[:, q * 128:(q + 1) * 128],
                         vtmp[:, q * 128:(q + 1) * 128], idb,
                         inc=[(sTM, 1)] if q == 3 else None)
                ttr = sTM.inc()
                k.wait("vector", sTM, ttr)
                if do_proj.v_st > 0:
                    k.wait("vector", sQS, do_proj.v_st)
                k.em("vector", "tensor_copy", vtok, p5[:, 0:512],
                     inc=[(sVE, 1)])
                do_proj.v_ev = sVE.inc()
                k.wait("scalar", sVE, do_proj.v_ev)
                dstv = vcc4[head_idx, tc * 4:(tc + 1) * 4].rearrange(
                    "a p d -> p a d")
                srcv = vtok.rearrange("p (a d) -> p a d", a=4)
                do_proj.v_st = k.dma("scalar", dstv, srcv, sQS)
            else:
                if tc == 0 and do_proj.acc_busy > 0:
                    k.wait("scalar", sQS, do_proj.acc_busy)
                k.em("scalar", "activation", accT[:, tc * 512:(tc + 1) * 512],
                     bank[:], AF.Copy, inc=[(sSE, 1)])
                do_proj.se_bank[(oc_idx * NTCH + tc) % 2] = sSE.inc()
        if is_v:
            return
        k.wait("scalar", sSE)
        k.dma("scalar", qsw[0:64, :], accT[64:128, :], sQS)
        swv = k.dma("scalar", qsw[64:128, :], accT[0:64, :], sQS)
        k.wait("vector", sQS, swv)
        k.em("vector", "tensor_tensor", ro_t1, qsw, ssS, ALU.mult,
             inc=[(sVE, 1)])
        sVE.inc()
        k.em("vector", "tensor_tensor", accT, accT, cxS, ALU.mult,
             inc=[(sVE, 1)])
        sVE.inc()
        k.wait("vector", sVE)
        k.em("vector", "tensor_tensor", accT, accT, ro_t1, ALU.add,
             inc=[(sVE, 1)])
        rope_done = sVE.inc()
        k.wait("scalar", sVE, rope_done)
        if is_q:
            for tc in range(NTCH):
                do_proj.acc_busy = k.dma(
                    "scalar", qscr[head_idx, tc],
                    accT[:, tc * 512:(tc + 1) * 512], sQS)
        else:
            for tc in range(NTCH):
                do_proj.acc_busy = k.dma(
                    "scalar", kcc4[head_idx, tc],
                    accT[:, tc * 512:(tc + 1) * 512], sQS)

    do_proj.tm_done = [0, 0]
    do_proj.se_bank = [0, 0]
    do_proj.acc_busy = 0
    do_proj.v_ev = 0
    do_proj.v_st = 0

    k.wait("vector", sQX, ldss)
    for g in range(NKV):
        do_proj(g, wk, g * 128, True, False, False, g)
    for g in range(NKV):
        do_proj(NKV + g, wv, g * 128, False, True, False, g)
    k.wait("gpsimd", sQS)
    k.em("gpsimd", "collective_compute", "AllGather", ALU.bypass,
         [[0, 1], [2, 3], [4, 5], [6, 7]],
         [kcc_in.ap().opt()], [kcc_out.ap().opt()], inc=[(sCK, 1)])
    sCK.inc()
    k.em("gpsimd", "collective_compute", "AllGather", ALU.bypass,
         [[0, 1], [2, 3], [4, 5], [6, 7]],
         [vcc_in.ap().opt()], [vcc_out.ap().opt()], inc=[(sCV, 1)])
    sCV.inc()
    k.fence()
    if upto == "A1":
        return k

    ln_build(xq, "q")
    ldcx = k.dma("sync", cxS, cxq.ap(), sQX)
    ldss = k.dma("sync", ssS, ssq.ap(), sQX)
    k.wait("vector", sQX, ldss)
    do_proj.tm_done = [0, 0]
    do_proj.se_bank = [0, 0]
    do_proj.acc_busy = 0
    for h in range(NH):
        do_proj(h, wq, h * 128, False, False, True, h)
    k.fence()
    if upto == "A2":
        return k

    Kg = [SD(0, BF16, 4096), SD(8192, BF16, 4096)]
    Vg = [SD(16384, BF16, 4096), SD(24576, BF16, 4096)]
    Qt = [SD(32768, BF16, 512), SD(33792, BF16, 512)]
    Pt = [SD(34816, BF16, 512), SD(35840, BF16, 512), SD(36864, BF16, 512),
          SD(53248, BF16, 512), SD(54272, BF16, 512)]
    rbS = SD(37888, F32, 512)
    ctxo = [SD(39936, BF16, 512), SD(40960, BF16, 512)]
    maskS = SD(43008, BF16, 8 * 512)
    recS = SD(51200, F32, 512, parts=1)

    mval = k.dma("sync", maskS.rearrange("p (a c) -> p a c", a=8),
                 masks.ap().rearrange("a p c -> p a c"), sQX)
    k.wait("vector", sQX, mval)
    k.wait("tensor", sCK, 1)
    k.wait("tensor", sCV, 1)
    k.wait("gpsimd", sCK, 1)
    k.wait("gpsimd", sCV, 1)

    kg_ld = [0, 0]
    kg_free = [0, 0]
    def load_g(g):
        buf = g % 2
        if kg_free[buf] > 0:
            k.wait("gpsimd", sTM, kg_free[buf])
        v = 0
        for r in range(2):
            for tcl in range(NTCH):
                v = k.dma("gpsimd", Kg[buf][:, (r * 4 + tcl) * 512:
                                            (r * 4 + tcl + 1) * 512],
                          kcc_o4[r, g, tcl], sQW)
            for q4 in range(4):
                dstv = Vg[buf][:, (r * 16 + q4 * 4) * 128:
                               (r * 16 + q4 * 4 + 4) * 128].rearrange(
                    "p (a d) -> p a d", a=4)
                srcv = vcc_o4[r, g, q4 * 4:(q4 + 1) * 4].rearrange(
                    "a p d -> p a d")
                v = k.dma("gpsimd", dstv, srcv, sQW)
        kg_ld[buf] = v

    load_g(0)
    qt_ld = [0, 0]
    qt_free = [0, 0]
    se_p = [0] * 5
    ve_p = [0] * 5
    tm_p = [0] * 5
    se_sbank = [0, 0]
    ve_ctx = [0, 0]
    qx_ctx = [0, 0]
    kcount = 0
    lnT = SD(57344, F32, 512, parts=1)

    def qt_load(it):
        h_, tc_ = (it // NTCH), it % NTCH
        qb_ = it % 2
        if qt_free[qb_] > 0:
            k.wait("sync", sTM, qt_free[qb_])
        qt_ld[qb_] = k.dma("sync", Qt[qb_], qscr[h_, tc_], sQX)

    ep_state = {
        "ep_prev": None,
        "tm_sum_prev": 0,
        "se_ln_prev": 0,
        "se_rbs_prev": 0,
        "ve_ctxev_prev": 0,
        "tm_bc_prev": 0,
    }

    def ep_head(it, tm_sum):
        k.wait("scalar", sTM, tm_sum)
        if ep_state["tm_bc_prev"] > 0:
            k.wait("scalar", sTM, ep_state["tm_bc_prev"])
        k.em("scalar", "activation", lnT, p7[0:1, :], AF.Ln, inc=[(sSE, 1)])
        ep_state["se_ln_prev"] = sSE.inc()
        k.wait("scalar", sSE)
        k.em("scalar", "activation", recS, lnT, AF.Exp, scale=-1.0,
             inc=[(sSE, 1)])
        return {"it": it, "se_exp": sSE.inc()}

    def ep_bcast(ep):
        k.wait("tensor", sSE, ep["se_exp"])
        if ep_state["se_rbs_prev"] > 0:
            k.wait("tensor", sSE, ep_state["se_rbs_prev"])
        k.em("tensor", "matmul", p6[:], onfr, recS,
             start=True, stop=True, inc=[(sTM, 1)])
        ep["tm_bc"] = sTM.inc()
        ep_state["tm_bc_prev"] = ep["tm_bc"]

    def ep_rbs(ep):
        k.wait("scalar", sTM, ep["tm_bc"])
        if ep_state["ve_ctxev_prev"] > 0:
            k.wait("scalar", sVE, ep_state["ve_ctxev_prev"])
        k.em("scalar", "activation", rbS, p6[:], AF.Copy, inc=[(sSE, 1)])
        ep["se_rbs"] = sSE.inc()
        ep_state["se_rbs_prev"] = ep["se_rbs"]

    def ep_ctxev(ep):
        it = ep["it"]
        k.wait("vector", sSE, ep["se_rbs"])
        if qx_ctx[it % 2] > 0:
            k.wait("vector", sQX, qx_ctx[it % 2])
        k.em("vector", "tensor_tensor", ctxo[it % 2], p[2 + it % 2][:], rbS,
             ALU.mult, inc=[(sVE, 1)])
        ve_ctx[it % 2] = sVE.inc()
        ep_state["ve_ctxev_prev"] = ve_ctx[it % 2]
        k.wait("sync", sVE, ve_ctx[it % 2])
        qx_ctx[it % 2] = k.dma("sync", ctxscr[it // NTCH, it % NTCH],
                               ctxo[it % 2], sQX)

    qt_load(0)
    for g in range(NKV):
        if g + 1 < NKV:
            load_g(g + 1)
        buf = g % 2
        k.wait("tensor", sQW, kg_ld[buf])
        for hh in range(4):
            h = 4 * g + hh
            for tc in range(NTCH):
                it = h * NTCH + tc
                qb = it % 2
                qv = qt_ld[qb]
                if it + 1 < NH * NTCH:
                    qt_load(it + 1)
                nkb = 8 * tc + 8
                cbank = p[2 + (it % 2)]
                k.wait("tensor", sQX, qv)

                def s_mm(kb):
                    sb_ = kb % 2
                    if se_sbank[sb_] > 0:
                        k.wait("tensor", sSE, se_sbank[sb_])
                    k.em("tensor", "matmul", p[sb_][:],
                         Kg[buf][:, kb * 128:(kb + 1) * 128], Qt[qb],
                         start=True, stop=True, inc=[(sTM, 1)])
                    v = sTM.inc()
                    if kb == nkb - 1:
                        qt_free[qb] = v
                    return v

                mm_s = [0] * nkb
                mm_s[0] = s_mm(0)
                for kb in range(nkb):
                    sl = kcount % 5
                    kcount += 1
                    sb_ = (kb % 2)
                    sbank = p[sb_]
                    if kb + 1 < nkb:
                        mm_s[kb + 1] = s_mm(kb + 1)
                    if kb == 4 and ep_state["ep_prev"] is not None:
                        ep_bcast(ep_state["ep_prev"])
                    if kb == 6 and ep_state["ep_prev"] is not None:
                        ep_rbs(ep_state["ep_prev"])
                        ep_ctxev(ep_state["ep_prev"])
                        ep_state["ep_prev"] = None
                    k.wait("scalar", sTM, mm_s[kb])
                    if tm_p[sl] > 0:
                        k.wait("scalar", sTM, tm_p[sl])
                    if ve_p[sl] > 0:
                        k.wait("scalar", sVE, ve_p[sl])
                    k.em("scalar", "activation", Pt[sl], sbank[:], AF.Exp,
                         scale=SCALE, inc=[(sSE, 1)])
                    se_p[sl] = sSE.inc()
                    se_sbank[sb_] = se_p[sl]
                    r = kb - 8 * tc
                    if r >= 0:
                        k.wait("vector", sSE, se_p[sl])
                        k.em("vector", "tensor_tensor", Pt[sl], Pt[sl],
                             maskS[:, r * 512:(r + 1) * 512], ALU.mult,
                             inc=[(sVE, 1)])
                        ve_p[sl] = sVE.inc()
                        k.wait("tensor", sVE, ve_p[sl])
                    else:
                        k.wait("tensor", sSE, se_p[sl])
                    if kb == 0 and ve_ctx[it % 2] > 0:
                        k.wait("tensor", sVE, ve_ctx[it % 2])
                    if kb == 0 and ep_state["se_ln_prev"] > 0:
                        k.wait("tensor", sSE, ep_state["se_ln_prev"])
                    k.em("tensor", "matmul", cbank[:],
                         Vg[buf][:, kb * 128:(kb + 1) * 128], Pt[sl],
                         start=(kb == 0), stop=(kb == nkb - 1),
                         skip_group_check=True)
                    k.em("tensor", "matmul", p7[0:1, :], onb, Pt[sl],
                         start=(kb == 0), stop=(kb == nkb - 1),
                         skip_group_check=True, inc=[(sTM, 1)])
                    tm_p[sl] = sTM.inc()
                if tc == NTCH - 1 and hh == 3:
                    kg_free[buf] = tm_p[(kcount - 1) % 5]
                ep_state["ep_prev"] = ep_head(it, tm_p[(kcount - 1) % 5])
    ep_bcast(ep_state["ep_prev"])
    ep_rbs(ep_state["ep_prev"])
    ep_ctxev(ep_state["ep_prev"])
    k.fence()
    if upto == "B":
        return k

    wos = [SD(0, BF16, 32 * 128), SD(8192, BF16, 32 * 128)]
    scb = [SD(16384, BF16, 512), SD(17408, BF16, 512)]
    gw1s = SD(18432, BF16, 4096, parts=32)
    gw2s = SD(26624, BF16, 32, parts=8)
    gvecS = SD(26752, F32, 104, parts=1)

    for h in range(NH):
        k.dma("sync",
              ctxT[:, h * 2048:(h + 1) * 2048].rearrange(
                  "p (a t) -> p a t", a=4),
              ctxscr[h].rearrange("a p t -> p a t"), sQX)
    ld_ctx = sQX.n

    ldg1 = k.dma("gpsimd", gw1s, gw1.ap(), sQW)
    ldg2 = k.dma("gpsimd", gw2s, gw2.ap(), sQW)
    ldgv = k.dma("gpsimd", gvecS, gvec.ap(), sQW)
    k.wait("tensor", sQW, ldg1)
    for grp in range(4):
        half = grp % 2
        k.wait("tensor", sVE)
        for q in range(8):
            i = grp * 8 + q
            k.em("tensor", "transpose",
                 p5[0:128, half * 512 + q * 32: half * 512 + (q + 1) * 32],
                 gw1s[:, i * 128:(i + 1) * 128], idb[0:32, 0:32],
                 inc=[(sTM, 1)] if q == 7 else None)
        tg = sTM.inc()
        k.wait("vector", sTM, tg)
        dst3 = gw1T.ap().rearrange("p (a c) -> p a c", c=32)[
            :, grp * 8:(grp + 1) * 8, :]
        src3 = p5[:, half * 512: half * 512 + 256].rearrange(
            "p (a c) -> p a c", a=8)
        k.em("vector", "tensor_copy", dst3, src3, inc=[(sVE, 1)])
        sVE.inc()
    k.wait("tensor", sQW, ldg2)
    k.wait("tensor", sVE)
    k.em("tensor", "transpose", p5[0:32, 0:8], gw2s, idb[0:8, 0:8],
         inc=[(sTM, 1)])
    tg2 = sTM.inc()
    k.wait("vector", sTM, tg2)
    k.em("vector", "tensor_copy", gw2T, p5[0:32, 0:8], inc=[(sVE, 1)])
    sVE.inc()
    k.wait("tensor", sQW, ldgv)
    k.em("tensor", "matmul", p6[:, 0:104], onfr, gvecS, start=True, stop=True,
         inc=[(sTM, 1)])
    tgv = sTM.inc()
    k.wait("vector", sTM, tgv)
    k.em("vector", "tensor_copy", gvb, p6[:, 0:104], inc=[(sVE, 1)])
    sVE.inc()

    if upto == "Cp":
        return k
    wo_tm = [0, 0]
    wo_se = [0, 0]
    wo_qs = [0, 0]
    nob_c = 2 if upto == "Cq" else 32
    nob_c = int(os.environ.get("TRNC_OBS", nob_c))
    ser_c = os.environ.get("TRNC_SERIAL", "") == "1"
    for ob in range(nob_c):
        buf = ob % 2
        if wo_tm[buf] > 0:
            k.wait("gpsimd", sTM, wo_tm[buf])
        src = wo.ap()[:, ob * 128:(ob + 1) * 128].rearrange(
            "(a p) c -> p a c", p=128)
        ldw = k.dma("gpsimd", wos[buf][:, :].rearrange("p (a c) -> p a c",
                                                       c=128), src, sQW)
        for tc in range(NTCH):
            it = ob * NTCH + tc
            bank = p[it % 2]
            if ser_c:
                k.fence()
            if wo_se[it % 2] > 0:
                k.wait("tensor", sSE, wo_se[it % 2])
            k.wait("tensor", sQW, ldw)
            k.wait("tensor", sQX, ld_ctx)
            for dv in range(32):
                k.em("tensor", "matmul", bank[:],
                     wos[buf][:, dv * 128:(dv + 1) * 128],
                     ctxT[:, dv * 2048 + tc * 512: dv * 2048 + tc * 512 + 512],
                     start=(dv == 0), stop=(dv == 31),
                     inc=[(sTM, 1)] if dv == 31 else None)
            mm = sTM.inc()
            if tc == NTCH - 1:
                wo_tm[buf] = mm
            k.wait("scalar", sTM, mm)
            if wo_qs[it % 2] > 0:
                k.wait("scalar", sQS, wo_qs[it % 2])
            k.em("scalar", "activation", scb[it % 2], bank[:], AF.Copy,
                 inc=[(sSE, 1)])
            wo_se[it % 2] = sSE.inc()
            k.wait("scalar", sSE, wo_se[it % 2])
            wo_qs[it % 2] = k.dma("scalar", aoscr[ob, tc], scb[it % 2], sQS)
    k.fence()
    if upto in ("C", "Cq"):
        return k

    aost2 = [SD(0, BF16, 4096), SD(57344, BF16, 4096)]
    aotm = SD(8192, BF16, 4096)
    x0f2 = SD(16384, F32, 4096)
    x1f = SD(32768, F32, 4096)
    xftm = SD(49152, BF16, 4096)
    hfb = k.sb("hfb", [128, 32], F32).ap()
    hbb = k.sb("hbb", [128, 32], BF16).ap()
    gfb = k.sb("gfb", [128, 8], F32).ap()
    m1b = k.sb("m1b", [128, 1], F32).ap()
    m2b = k.sb("m2b", [128, 1], F32).ap()
    gml = k.sb("gml", [128, 8], F32).ap()
    sgd = k.sb("sgd", [128, 1], F32).ap()
    wjf = k.sb("wjf", [128, 8], F32).ap()
    wjf2 = k.sb("wjf2", [128, 8], F32).ap()
    wjb = k.sb("wjb", [128, 8], BF16).ap()

    tm_tr_prev = 0
    tm_tr_prev2 = 0
    qw_x1_prev = 0
    pl_add_prev = 0
    for j in range(NTB):
        tcj, sub = j // 4, j % 4
        aost = aost2[j % 2]
        if j > 1:
            k.wait("sync", sTM, tm_tr_prev2)
        if j > 0:
            k.wait("sync", sPL, pl_add_prev)
        lda = k.dma("sync",
                    aost.rearrange("p (a t) -> p a t", a=32),
                    aoscr[0:32, tcj, :, sub * 128:(sub + 1) * 128]
                    .rearrange("a p t -> p a t"), sQX)
        ldx = k.dma("sync", x0f2, xq.ap()[j * 128:(j + 1) * 128, :], sQX)
        k.wait("tensor", sQX, lda)
        for grp in range(8):
            half = grp % 2
            k.wait("tensor", sVE)
            for q in range(4):
                i = grp * 4 + q
                k.em("tensor", "transpose",
                     p5[:, half * 512 + q * 128: half * 512 + (q + 1) * 128],
                     aost[:, i * 128:(i + 1) * 128], idb,
                     inc=[(sTM, 1)] if q == 3 else None)
            ttr = sTM.inc()
            k.wait("vector", sTM, ttr)
            dst3 = aotm.rearrange("p (a t) -> p a t", a=32)[
                :, grp * 4:(grp + 1) * 4, :]
            src3 = p5[:, half * 512:(half + 1) * 512].rearrange(
                "p (a t) -> p a t", a=4)
            k.em("vector", "tensor_copy", dst3, src3, inc=[(sVE, 1)])
            sVE.inc()
        tm_tr_prev2 = tm_tr_prev
        tm_tr_prev = sTM.n
        k.wait("scalar", sVE)
        k.em("scalar", "activation", xftm, aotm, AF.Square, accum_out=S2t,
             inc=[(sSE, 1)])
        se_sq = sSE.inc()
        k.em("vector", "reduce_sum", S1t, aotm, axis=AXX, inc=[(sVE, 1)])
        sVE.inc()
        k.wait("vector", sVE)
        k.em("vector", "tensor_scalar", Mt, S1t, INV_D, None, ALU.mult,
             inc=[(sVE, 1)])
        sVE.inc()
        k.wait("vector", sVE)
        k.em("vector", "tensor_scalar", MSQt, Mt, Mt, None, ALU.mult,
             inc=[(sVE, 1)])
        sVE.inc()
        k.wait("vector", sSE, se_sq)
        k.em("vector", "tensor_scalar", VARt, S2t, INV_D, None, ALU.mult)
        k.wait("vector", sVE)
        k.em("vector", "tensor_tensor", VARt, VARt, MSQt, ALU.subtract,
             inc=[(sVE, 1)])
        sVE.inc()
        k.wait("scalar", sVE)
        k.em("scalar", "activation", SQt, VARt, AF.Sqrt, bias=epsS,
             inc=[(sSE, 1)])
        sSE.inc()
        k.wait("vector", sSE)
        k.em("vector", "reciprocal", RSt, SQt, inc=[(sVE, 1)])
        sVE.inc()
        k.wait("vector", sVE)
        k.em("vector", "tensor_scalar", NBt, Mt, RSt, -1.0, ALU.mult,
             ALU.mult, inc=[(sVE, 1)])
        ve_nb = sVE.inc()
        k.wait("scalar", sVE, ve_nb)
        if qw_x1_prev > 0:
            k.wait("scalar", sQW, qw_x1_prev)
        if pl_add_prev > 0:
            k.wait("scalar", sPL, pl_add_prev)
        k.em("scalar", "activation", x1f, aotm, AF.Identity, bias=NBt,
             scale=RSt, inc=[(sSE, 1)])
        se_t1 = sSE.inc()
        k.wait("gpsimd", sSE, se_t1)
        k.wait("gpsimd", sQX, ldx)
        k.em("gpsimd", "tensor_tensor", x1f, x1f, x0f2, ALU.add,
             inc=[(sPL, 1)])
        pl_add_prev = sPL.inc()
        qw_x1_prev = k.dma("gpsimd", x1scr[j], x1f, sQW)
        k.wait("scalar", sPL, pl_add_prev)
        k.em("scalar", "activation", aotm, x1f, AF.Square, accum_out=S2t,
             inc=[(sSE, 1)])
        se_sq2 = sSE.inc()
        k.wait("vector", sVE)
        k.em("vector", "tensor_copy", Mt, x0m.ap()[:, j:j + 1],
             inc=[(sVE, 1)])
        sVE.inc()
        k.wait("vector", sVE)
        k.em("vector", "tensor_scalar", MSQt, Mt, Mt, None, ALU.mult,
             inc=[(sVE, 1)])
        sVE.inc()
        k.wait("vector", sSE, se_sq2)
        k.em("vector", "tensor_scalar", VARt, S2t, INV_D, None, ALU.mult)
        k.wait("vector", sVE)
        k.em("vector", "tensor_tensor", VARt, VARt, MSQt, ALU.subtract,
             inc=[(sVE, 1)])
        sVE.inc()
        k.wait("scalar", sVE)
        k.em("scalar", "activation", SQt, VARt, AF.Sqrt, bias=epsS,
             inc=[(sSE, 1)])
        sSE.inc()
        k.wait("vector", sSE)
        k.em("vector", "reciprocal", RSt, SQt, inc=[(sVE, 1)])
        sVE.inc()
        k.wait("vector", sVE)
        k.em("vector", "tensor_scalar", NBt, Mt, RSt, -1.0, ALU.mult,
             ALU.mult, inc=[(sVE, 1)])
        ve_nb2 = sVE.inc()
        k.wait("scalar", sVE, ve_nb2)
        k.em("scalar", "activation", xftm, x1f, AF.Identity, bias=NBt,
             scale=RSt, inc=[(sSE, 1)])
        se_xf = sSE.inc()
        for grp in range(8):
            half = grp % 2
            k.wait("tensor", sVE)
            k.wait("tensor", sSE, se_xf)
            for q in range(4):
                i = grp * 4 + q
                k.em("tensor", "transpose",
                     p5[:, half * 512 + q * 128: half * 512 + (q + 1) * 128],
                     xftm[:, i * 128:(i + 1) * 128], idb,
                     inc=[(sTM, 1)] if q == 3 else None)
            ttr = sTM.inc()
            k.wait("vector", sTM, ttr)
            dst3 = xfT.rearrange("p (a t) -> p a t", t=2048)[
                :, grp * 4:(grp + 1) * 4, j * 128:(j + 1) * 128]
            src3 = p5[:, half * 512:(half + 1) * 512].rearrange(
                "p (a t) -> p a t", a=4)
            k.em("vector", "tensor_copy", dst3, src3, inc=[(sVE, 1)])
            sVE.inc()
        ve_xfT = sVE.n
        k.wait("tensor", sVE, ve_xfT)
        for i in range(32):
            k.em("tensor", "matmul", p6[:, 0:32],
                 xfT.rearrange("p (a t) -> p a t", t=2048)[
                     :, i, j * 128:(j + 1) * 128],
                 gw1T.ap().rearrange("p (a c) -> p a c", c=32)[:, i, :],
                 start=(i == 0), stop=(i == 31),
                 inc=[(sTM, 1)] if i == 31 else None)
        mm_h = sTM.inc()
        k.wait("vector", sTM, mm_h)
        k.em("vector", "tensor_tensor", hfb, p6[:, 0:32], gvb[:, 0:32],
             ALU.add, inc=[(sVE, 1)])
        sVE.inc()
        k.wait("vector", sVE)
        k.em("vector", "reduce_sum", S1t, hfb, axis=AXX, inc=[(sVE, 1)])
        sVE.inc()
        k.wait("vector", sVE)
        k.em("vector", "tensor_scalar", Mt, S1t, 1.0 / 32.0, None, ALU.mult,
             inc=[(sVE, 1)])
        sVE.inc()
        k.wait("vector", sVE)
        k.em("vector", "tensor_scalar", MSQt, Mt, Mt, None, ALU.mult,
             inc=[(sVE, 1)])
        sVE.inc()
        k.wait("scalar", sVE)
        k.em("scalar", "activation", hbb, hfb, AF.Square, accum_out=S2t,
             inc=[(sSE, 1)])
        sSE.inc()
        k.wait("vector", sSE)
        k.em("vector", "tensor_scalar", VARt, S2t, 1.0 / 32.0, None, ALU.mult)
        k.wait("vector", sVE)
        k.em("vector", "tensor_tensor", VARt, VARt, MSQt, ALU.subtract,
             inc=[(sVE, 1)])
        sVE.inc()
        k.wait("scalar", sVE)
        k.em("scalar", "activation", SQt, VARt, AF.Sqrt, bias=epsS,
             inc=[(sSE, 1)])
        sSE.inc()
        k.wait("vector", sSE)
        k.em("vector", "reciprocal", RSt, SQt, inc=[(sVE, 1)])
        sVE.inc()
        k.wait("vector", sVE)
        k.em("vector", "tensor_scalar", hfb, hfb, Mt, RSt, ALU.subtract,
             ALU.mult, inc=[(sVE, 1)])
        sVE.inc()
        k.wait("vector", sVE)
        k.em("vector", "tensor_tensor", hfb, hfb, gvb[:, 32:64], ALU.mult,
             inc=[(sVE, 1)])
        sVE.inc()
        k.wait("vector", sVE)
        k.em("vector", "tensor_tensor", hfb, hfb, gvb[:, 64:96], ALU.add,
             inc=[(sVE, 1)])
        sVE.inc()
        k.wait("vector", sVE)
        k.em("vector", "tensor_scalar", hbb, hfb, 0.0, None, ALU.max,
             inc=[(sVE, 1)])
        ve_h = sVE.inc()
        k.wait("tensor", sVE, ve_h)
        k.em("tensor", "transpose", p5[0:32, 512:640], hbb, idb,
             inc=[(sTM, 1)])
        tm_ht = sTM.inc()
        k.wait("vector", sTM, tm_ht)
        k.em("vector", "tensor_copy", wTall.ap()[0:32, 0:128],
             p5[0:32, 512:640], inc=[(sVE, 1)])
        ve_ht = sVE.inc()
        k.wait("tensor", sVE, ve_ht)
        k.em("tensor", "matmul", p6[:, 64:72], wTall.ap()[0:32, 0:128],
             gw2T, start=True, stop=True, inc=[(sTM, 1)])
        mm_g = sTM.inc()
        k.wait("vector", sTM, mm_g)
        k.em("vector", "tensor_tensor", gfb, p6[:, 64:72], gvb[:, 96:104],
             ALU.add, inc=[(sVE, 1)])
        sVE.inc()
        k.wait("vector", sVE)
        k.em("vector", "reduce_max", m1b, gfb, axis=AXX, inc=[(sVE, 1)])
        sVE.inc()
        k.wait("vector", sVE)
        k.em("vector", "tensor_scalar", issb.ap()[:, j * 16: j * 16 + 8],
             gfb, m1b, None, ALU.is_equal, inc=[(sVE, 1)])
        sVE.inc()
        k.wait("vector", sVE)
        k.em("vector", "tensor_scalar", gml,
             issb.ap()[:, j * 16: j * 16 + 8], -1e30, None, ALU.mult,
             inc=[(sVE, 1)])
        sVE.inc()
        k.wait("vector", sVE)
        k.em("vector", "tensor_tensor", gml, gml, gfb, ALU.add,
             inc=[(sVE, 1)])
        sVE.inc()
        k.wait("vector", sVE)
        k.em("vector", "reduce_max", m2b, gml, axis=AXX, inc=[(sVE, 1)])
        sVE.inc()
        k.wait("vector", sVE)
        k.em("vector", "tensor_scalar", issb.ap()[:, j * 16 + 8: j * 16 + 16],
             gml, m2b, None, ALU.is_equal, inc=[(sVE, 1)])
        sVE.inc()
        k.wait("vector", sVE)
        k.em("vector", "tensor_tensor", sgd, m1b, m2b, ALU.subtract,
             inc=[(sVE, 1)])
        ve_sgd = sVE.inc()
        k.wait("scalar", sVE, ve_sgd)
        k.em("scalar", "activation", g12.ap()[:, 2 * j: 2 * j + 1], sgd,
             AF.Sigmoid, inc=[(sSE, 1)])
        se_g1 = sSE.inc()
        k.wait("vector", sSE, se_g1)
        k.em("vector", "tensor_scalar", g12.ap()[:, 2 * j + 1: 2 * j + 2],
             g12.ap()[:, 2 * j: 2 * j + 1], -1.0, 1.0, ALU.mult, ALU.add,
             inc=[(sVE, 1)])
        sVE.inc()
        k.wait("tensor", sVE)
        k.em("tensor", "matmul", p7[0:1, 0:16], onb,
             issb.ap()[:, j * 16:(j + 1) * 16],
             start=(j == 0), stop=(j == NTB - 1), skip_group_check=True,
             inc=[(sTM, 1)])
        sTM.inc()
    k.wait("vector", sTM)
    k.em("vector", "tensor_copy", cntS, p7[0:1, 0:16], inc=[(sVE, 1)])
    ve_cnt = sVE.inc()
    k.wait("gpsimd", sVE, ve_cnt)
    k.dma("gpsimd", cnt_in.ap(), cntS, sQW)
    k.wait("gpsimd", sQW)
    k.em("gpsimd", "collective_compute", "AllReduce", ALU.add,
         [[0, 1, 2, 3, 4, 5, 6, 7]],
         [cnt_in.ap().opt()], [cnt_out.ap().opt()], inc=[(sCN, 1)])
    sCN.inc()
    k.fence()
    if upto == "C2":
        return k

    asb = SD(0, BF16, 4096)
    bsb = SD(8192, BF16, 4096)
    tmpT = SD(16384, F32, 2048)
    scT = SD(24576, BF16, 2048)
    weT = SD(28672, BF16, 2048)
    wband = [SD(32768, BF16, 4096), SD(40960, BF16, 4096)]
    WTb = [SD(16384, BF16, 4096), SD(0, BF16, 4096)]
    fob = [SD(49152, BF16, 512), SD(50176, BF16, 512)]

    lda_ = k.dma("gpsimd", asb.rearrange("p (a c) -> p a c", c=128),
                 astack.ap().rearrange("(a p) c -> p a c", p=128), sQW)
    ldb_ = k.dma("gpsimd", bsb, bstack.ap(), sQW)
    for tc in range(NTCH):
        bank = p[tc % 2]
        k.wait("tensor", sQW, lda_)
        k.wait("tensor", sVE)
        for i in range(32):
            k.em("tensor", "matmul", bank[:],
                 asb.rearrange("p (a c) -> p a c", c=128)[:, i, :],
                 xfT.rearrange("p (a t) -> p a t", t=2048)[
                     :, i, tc * 512:(tc + 1) * 512],
                 start=(i == 0), stop=(i == 31),
                 inc=[(sTM, 1)] if i == 31 else None)
        mm = sTM.inc()
        k.wait("vector", sTM, mm)
        k.em("vector", "tensor_copy", tmpT[:, tc * 512:(tc + 1) * 512],
             bank[:], inc=[(sVE, 1)])
        sVE.inc()
    k.wait("sync", sCN, 1)
    ldc = k.dma("sync", cntS, cnt_out.ap(), sQX)
    k.wait("vector", sQX, ldc)
    k.em("vector", "tensor_scalar", alwS, cntS, CAPACITY, None, ALU.is_le,
         inc=[(sVE, 1)])
    ve_alw = sVE.inc()
    k.wait("tensor", sVE, ve_alw)
    k.em("tensor", "matmul", p6[:, 0:16], onfr, alwS, start=True, stop=True,
         inc=[(sTM, 1)])
    tm_alw = sTM.inc()
    k.wait("vector", sTM, tm_alw)
    k.em("vector", "tensor_copy", alwB, p6[:, 0:16], inc=[(sVE, 1)])
    sVE.inc()
    for j in range(NTB):
        k.wait("vector", sVE)
        k.em("vector", "tensor_scalar", wjf,
             issb.ap()[:, j * 16: j * 16 + 8],
             g12.ap()[:, 2 * j: 2 * j + 1], None, ALU.mult, inc=[(sVE, 1)])
        sVE.inc()
        k.wait("vector", sVE)
        k.em("vector", "tensor_tensor", wjf, wjf, alwB[:, 0:8], ALU.mult,
             inc=[(sVE, 1)])
        sVE.inc()
        k.em("vector", "tensor_scalar", wjf2,
             issb.ap()[:, j * 16 + 8: j * 16 + 16],
             g12.ap()[:, 2 * j + 1: 2 * j + 2], None, ALU.mult,
             inc=[(sVE, 1)])
        sVE.inc()
        k.wait("vector", sVE)
        k.em("vector", "tensor_tensor", wjf2, wjf2, alwB[:, 8:16], ALU.mult,
             inc=[(sVE, 1)])
        sVE.inc()
        k.wait("vector", sVE)
        k.em("vector", "tensor_tensor", wjb, wjf, wjf2, ALU.add,
             inc=[(sVE, 1)])
        ve_wj = sVE.inc()
        k.wait("tensor", sVE, ve_wj)
        k.em("tensor", "transpose", p5[0:8, 512:640], wjb, idb,
             inc=[(sTM, 1)])
        tm_wt = sTM.inc()
        k.wait("vector", sTM, tm_wt)
        k.em("vector", "tensor_copy",
             wTall.ap()[0:8, j * 128:(j + 1) * 128], p5[0:8, 512:640],
             inc=[(sVE, 1)])
        sVE.inc()
    ve_wtall = sVE.n
    for tc in range(NTCH):
        k.wait("tensor", sVE)
        k.em("tensor", "matmul", p6[:], erp,
             wTall.ap()[0:8, tc * 512:(tc + 1) * 512], start=True, stop=True,
             inc=[(sTM, 1)])
        tm_we = sTM.inc()
        k.wait("vector", sTM, tm_we)
        k.em("vector", "tensor_copy", weT[:, tc * 512:(tc + 1) * 512],
             p6[:], inc=[(sVE, 1)])
        sVE.inc()
    k.wait("vector", sVE)
    k.em("vector", "tensor_tensor", scT, tmpT, weT, ALU.mult, inc=[(sVE, 1)])
    ve_sc = sVE.inc()

    d3_tm = [0, 0]
    d3_wt = [0, 0]
    d3_se = [0, 0]
    d3_qs = [0, 0]
    for ob in range(32):
        buf = ob % 2
        if d3_tm[buf] > 0:
            k.wait("gpsimd", sTM, d3_tm[buf])
        ldw = k.dma("gpsimd", wband[buf],
                    wffn.ap()[ob * 128:(ob + 1) * 128, :], sQW)
        k.wait("tensor", sQW, ldw)
        if ob >= 2:
            k.wait("tensor", sVE, d3_wt[buf])
        for grp in range(8):
            half = grp % 2
            k.wait("tensor", sVE)
            for q in range(4):
                i = grp * 4 + q
                k.em("tensor", "transpose",
                     p5[:, half * 512 + q * 128: half * 512 + (q + 1) * 128],
                     wband[buf][:, i * 128:(i + 1) * 128], idb,
                     inc=[(sTM, 1)] if q == 3 else None)
            ttr = sTM.inc()
            k.wait("vector", sTM, ttr)
            dst3 = WTb[buf].rearrange("p (a c) -> p a c", c=128)[
                :, grp * 4:(grp + 1) * 4, :]
            src3 = p5[:, half * 512:(half + 1) * 512].rearrange(
                "p (a c) -> p a c", a=4)
            k.em("vector", "tensor_copy", dst3, src3, inc=[(sVE, 1)])
            sVE.inc()
        d3_wt[buf] = sVE.n
        k.wait("tensor", sVE, d3_wt[buf])
        if ob == 0:
            k.wait("tensor", sVE, ve_sc)
            k.wait("tensor", sQW, ldb_)
        for tc in range(NTCH):
            it = ob * NTCH + tc
            bank = p[2 + it % 2]
            if d3_se[it % 2] > 0:
                k.wait("tensor", sSE, d3_se[it % 2])
            for i in range(32):
                k.em("tensor", "matmul", bank[:],
                     WTb[buf].rearrange("p (a c) -> p a c", c=128)[:, i, :],
                     xfT.rearrange("p (a t) -> p a t", t=2048)[
                         :, i, tc * 512:(tc + 1) * 512],
                     start=(i == 0), stop=False)
            k.em("tensor", "matmul", bank[:],
                 bsb[:, ob * 128:(ob + 1) * 128],
                 scT[:, tc * 512:(tc + 1) * 512],
                 start=False, stop=True, skip_group_check=True,
                 inc=[(sTM, 1)])
            mm = sTM.inc()
            if tc == NTCH - 1:
                d3_tm[buf] = mm
            k.wait("scalar", sTM, mm)
            if d3_qs[it % 2] > 0:
                k.wait("scalar", sQS, d3_qs[it % 2])
            k.em("scalar", "activation", fob[it % 2], bank[:], AF.Copy,
                 inc=[(sSE, 1)])
            d3_se[it % 2] = sSE.inc()
            k.wait("scalar", sSE, d3_se[it % 2])
            d3_qs[it % 2] = k.dma("scalar", foscr[ob, tc], fob[it % 2], sQS)
    k.fence()
    if upto == "D":
        return k

    fost2 = [SD(0, BF16, 4096), SD(49152, BF16, 4096)]
    fotm = SD(8192, BF16, 4096)
    x1bf = [SD(16384, BF16, 4096), SD(24576, BF16, 4096)]
    outb = SD(32768, F32, 4096)

    qx_out = 0
    pl_out = [0, 0]
    se_sq_prev = 0
    tm_tr_prevE = 0
    tm_tr_prevE2 = 0
    se_sq_prev2 = 0
    for j in range(NTB):
        tcj, sub = j // 4, j % 4
        fost = fost2[j % 2]
        if j > 1:
            k.wait("sync", sTM, tm_tr_prevE2)
            k.wait("sync", sSE, se_sq_prev2)
        ldf = k.dma("sync",
                    fost.rearrange("p (a t) -> p a t", a=32),
                    foscr[0:32, tcj, :, sub * 128:(sub + 1) * 128]
                    .rearrange("a p t -> p a t"), sQX)
        if pl_out[j % 2] > 0:
            k.wait("gpsimd", sPL, pl_out[j % 2])
        ldx1 = k.dma("gpsimd", x1bf[j % 2], x1scr[j], sQW)
        k.wait("tensor", sQX, ldf)
        for grp in range(8):
            half = grp % 2
            k.wait("tensor", sVE)
            for q in range(4):
                i = grp * 4 + q
                k.em("tensor", "transpose",
                     p5[:, half * 512 + q * 128: half * 512 + (q + 1) * 128],
                     fost[:, i * 128:(i + 1) * 128], idb,
                     inc=[(sTM, 1)] if q == 3 else None)
            ttr = sTM.inc()
            k.wait("vector", sTM, ttr)
            dst3 = fotm.rearrange("p (a t) -> p a t", a=32)[
                :, grp * 4:(grp + 1) * 4, :]
            src3 = p5[:, half * 512:(half + 1) * 512].rearrange(
                "p (a t) -> p a t", a=4)
            k.em("vector", "tensor_copy", dst3, src3, inc=[(sVE, 1)])
            sVE.inc()
        tm_tr_prevE2 = tm_tr_prevE
        tm_tr_prevE = sTM.n
        k.wait("scalar", sVE)
        k.em("scalar", "activation", fost, fotm, AF.Square, accum_out=S2t,
             inc=[(sSE, 1)])
        se_sq = sSE.inc()
        se_sq_prev2 = se_sq_prev
        se_sq_prev = se_sq
        k.em("vector", "reduce_sum", S1t, fotm, axis=AXX, inc=[(sVE, 1)])
        sVE.inc()
        k.wait("vector", sVE)
        k.em("vector", "tensor_scalar", Mt, S1t, INV_D, None, ALU.mult,
             inc=[(sVE, 1)])
        sVE.inc()
        k.wait("vector", sVE)
        k.em("vector", "tensor_scalar", MSQt, Mt, Mt, None, ALU.mult,
             inc=[(sVE, 1)])
        sVE.inc()
        k.wait("vector", sSE, se_sq)
        k.em("vector", "tensor_scalar", VARt, S2t, INV_D, None, ALU.mult)
        k.wait("vector", sVE)
        k.em("vector", "tensor_tensor", VARt, VARt, MSQt, ALU.subtract,
             inc=[(sVE, 1)])
        sVE.inc()
        k.wait("scalar", sVE)
        k.em("scalar", "activation", SQt, VARt, AF.Sqrt, bias=epsS,
             inc=[(sSE, 1)])
        sSE.inc()
        k.wait("vector", sSE)
        k.em("vector", "reciprocal", RSt, SQt, inc=[(sVE, 1)])
        sVE.inc()
        k.wait("vector", sVE)
        k.em("vector", "tensor_scalar", NBt, Mt, RSt, -1.0, ALU.mult,
             ALU.mult, inc=[(sVE, 1)])
        ve_nb = sVE.inc()
        k.wait("scalar", sVE, ve_nb)
        if qx_out > 0:
            k.wait("scalar", sQX, qx_out)
        k.em("scalar", "activation", outb, fotm, AF.Identity, bias=NBt,
             scale=RSt, inc=[(sSE, 1)])
        se_t1 = sSE.inc()
        k.wait("gpsimd", sSE, se_t1)
        k.wait("gpsimd", sQW, ldx1)
        k.em("gpsimd", "tensor_tensor", outb, outb, x1bf[j % 2], ALU.add,
             inc=[(sPL, 1)])
        pl_out[j % 2] = sPL.inc()
        k.wait("sync", sPL, pl_out[j % 2])
        qx_out = k.dma("sync", out_d.ap()[j * 128:(j + 1) * 128, :],
                       outb, sQX)
    k.fence()

    return k


def _host_constants(s):
    bf = ml_dtypes.bfloat16
    qblocks = np.array([2 * i + s for i in range(NTB)])
    qpos = (qblocks[:, None] * 128 + np.arange(128)[None, :]).reshape(-1)
    kpos = np.arange(TPC) + 2048 * s
    inv = 1.0 / (500000.0 ** (np.arange(0, 128, 2, dtype=np.float64) / 128.0))

    def tabs(pos):
        fr = pos[:, None].astype(np.float64) * inv[None, :]
        c = np.cos(fr).T.astype(np.float32)
        sn = np.sin(fr).T.astype(np.float32)
        cx = np.vstack([c, c]).astype(bf)
        ss = np.vstack([-sn, sn]).astype(bf)
        return cx, ss

    cxq, ssq = tabs(qpos)
    cxk, ssk = tabs(kpos)
    kap = np.arange(128)[:, None]
    masks = np.zeros((8, 128, 512), dtype=np.float32)
    for r in range(8):
        for j in range(4):
            tau = np.arange(128)[None, :]
            valid = (128 * r + kap) <= (128 * (2 * j + s) + tau)
            masks[r, :, j * 128:(j + 1) * 128] = valid
    return {
        "cxq": np.ascontiguousarray(cxq), "ssq": np.ascontiguousarray(ssq),
        "cxk": np.ascontiguousarray(cxk), "ssk": np.ascontiguousarray(ssk),
        "masks": masks.astype(bf),
        "identb": np.eye(128, dtype=bf),
        "onesb": np.ones((128, 1), dtype=bf),
        "onesfr": np.ones((1, 128), dtype=np.float32),
        "epsd": np.full((128, 1), 1e-5, dtype=np.float32),
        "erep": (np.repeat(np.eye(8, dtype=np.float32), 16, axis=1)
                 / 16.0).astype(bf),
    }, qblocks


def kernel(L0_attn_x, L0_attn_Wq, L0_attn_Wk, L0_attn_Wv, L0_attn_Wo,
           L0_ffn_weight, L0_ffn_lora_A, L0_ffn_lora_B, L0_ffn_gw1,
           L0_ffn_gb1, L0_ffn_ln_gamma, L0_ffn_ln_beta, L0_ffn_gw2,
           L0_ffn_gb2):
    x = np.ascontiguousarray(np.asarray(L0_attn_x, dtype=np.float32))
    wq_ = np.ascontiguousarray(np.asarray(L0_attn_Wq, dtype=np.float32))
    wk_ = np.ascontiguousarray(np.asarray(L0_attn_Wk, dtype=np.float32))
    wv_ = np.ascontiguousarray(np.asarray(L0_attn_Wv, dtype=np.float32))
    wo_ = np.ascontiguousarray(np.asarray(L0_attn_Wo, dtype=np.float32))
    wf_ = np.ascontiguousarray(np.asarray(L0_ffn_weight, dtype=np.float32))
    astack = np.ascontiguousarray(
        np.asarray(L0_ffn_lora_A, dtype=np.float32).transpose(1, 0, 2)
        .reshape(D, 128))
    bstack = np.ascontiguousarray(
        np.asarray(L0_ffn_lora_B, dtype=np.float32).reshape(128, D))
    gw1_ = np.ascontiguousarray(np.asarray(L0_ffn_gw1, dtype=np.float32))
    gw2_ = np.ascontiguousarray(np.asarray(L0_ffn_gw2, dtype=np.float32))
    gvec = np.concatenate([
        np.asarray(L0_ffn_gb1, dtype=np.float32).ravel(),
        np.asarray(L0_ffn_ln_gamma, dtype=np.float32).ravel(),
        np.asarray(L0_ffn_ln_beta, dtype=np.float32).ravel(),
        np.asarray(L0_ffn_gb2, dtype=np.float32).ravel()])[None, :]
    gvec = np.ascontiguousarray(gvec)

    kb = build_program()
    nc = kb.finalize()

    in_maps = []
    qb_by_s = {}
    for c in range(NCORES):
        b, s = c // 2, c % 2
        consts, qblocks = _host_constants(s)
        qb_by_s[s] = qblocks
        xb = x[b].reshape(32, 128, D)
        m = {
            "xq": np.ascontiguousarray(xb[qblocks].reshape(TPC, D)),
            "xkv": np.ascontiguousarray(x[b, 2048 * s: 2048 * s + TPC]),
            "wq": wq_, "wk": wk_, "wv": wv_, "wo": wo_, "wffn": wf_,
            "astack": astack, "bstack": bstack,
            "gw1": gw1_, "gw2": gw2_, "gvec": gvec,
        }
        m.update(consts)
        in_maps.append(m)

    res = run_bass_kernel_spmd(nc, in_maps, core_ids=list(range(NCORES)))

    out = np.empty((B, T, D), dtype=np.float32)
    for c in range(NCORES):
        b, s = c // 2, c % 2
        shard = res.results[c]["out"].reshape(NTB, 128, D)
        ob = out[b].reshape(32, 128, D)
        ob[qb_by_s[s]] = shard
    return out
